# version 10
# baseline (speedup 1.0000x reference)
import numpy as np
import ml_dtypes
from contextlib import ExitStack

import concourse.bass as bass
import concourse.mybir as mybir
from concourse.bass_utils import run_bass_kernel_spmd
from concourse.alu_op_type import AluOpType

F32 = mybir.dt.float32
BF16 = mybir.dt.bfloat16

def _act_set_id(name):
    from concourse.hw_specs import get_activation_tables
    import concourse.bass as _b
    _nc = _b.Bass("TRN2", target_bir_lowering=False)
    return list(get_activation_tables(_nc.m.arch)).index(name)

LNEXP_SET = None
SILU_SET = None
AF = mybir.ActivationFunctionType
AX = mybir.AxisListType.X

H = 8
DH = 128
DM = 1024
DC = 8
HF = 1408
HC = 11
NE = 9
EPS_LN = 1e-5
EPS_RMS = 1e-6
NEG_BIG = -1.0e30

W13_SL = [(h, min(h + 2, HC)) for h in range(0, HC, 2)]
W2_GR = [(0, 3), (3, 6), (6, 9)]


class Ctr:

    def __init__(self, ctx, nc, name):
        self.h = ctx.enter_context(nc.semaphore(name))
        self.n = 0

    def inc(self, instr, k=1):
        instr.then_inc(self.h, k)
        self.n += k
        return instr

    def wait(self, eng, n=None):
        eng.wait_ge(self.h, self.n if n is None else n)


def build_nc(T, debug=False):
    from concourse.hw_specs import get_activation_tables
    NT = T // 128
    NQ = T // 512
    nc = bass.Bass("TRN2", target_bir_lowering=False, debug=debug)
    tab_names = list(get_activation_tables(nc.m.arch))
    LNEXP_SET = tab_names.index("natural_log_exp_and_others")
    SILU_SET = tab_names.index("silu_and_others")

    x_in = nc.dram_tensor("x", [T, DM], F32, kind="ExternalInput").ap()
    wq_in = nc.dram_tensor("wq", [DM, DM], F32, kind="ExternalInput").ap()
    wk_in = nc.dram_tensor("wk", [DM, DM], F32, kind="ExternalInput").ap()
    wv_in = nc.dram_tensor("wv", [DM, DM], F32, kind="ExternalInput").ap()
    wo_in = nc.dram_tensor("wo", [H, DH, DM], F32, kind="ExternalInput").ap()
    w13_in = nc.dram_tensor("w13", [NE, HC, 128, 2, DC, 128], F32, kind="ExternalInput").ap()
    w2_in = nc.dram_tensor("w2", [DC, 128, NE, HC, 128], F32, kind="ExternalInput").ap()
    b13_in = nc.dram_tensor("b13", [128, NE, HC, 2], F32, kind="ExternalInput").ap()
    b2_in = nc.dram_tensor("b2", [8, DM], F32, kind="ExternalInput").ap()
    sb2_in = nc.dram_tensor("sb2c", [128, DC], F32, kind="ExternalInput").ap()
    gwt_in = nc.dram_tensor("gwt", [128, DC, 8], F32, kind="ExternalInput").ap()
    gbb_in = nc.dram_tensor("gbb", [128, 8], F32, kind="ExternalInput").ap()
    cos_in = nc.dram_tensor("cosT", [NT, 128, 64], F32, kind="ExternalInput").ap()
    sin_in = nc.dram_tensor("sinT", [NT, 128, 64], F32, kind="ExternalInput").ap()
    identb_in = nc.dram_tensor("identb", [128, 128], BF16, kind="ExternalInput").ap()
    identf_in = nc.dram_tensor("identf", [128, 128], F32, kind="ExternalInput").ap()
    ones1b_in = nc.dram_tensor("ones1b", [1, 128], BF16, kind="ExternalInput").ap()
    onescb_in = nc.dram_tensor("onescb", [128, 1], BF16, kind="ExternalInput").ap()
    ones1f_in = nc.dram_tensor("ones1f", [1, 128], F32, kind="ExternalInput").ap()
    masks_in = nc.dram_tensor("masksb", [128, 4, 512], BF16, kind="ExternalInput").ap()
    sel8_in = nc.dram_tensor("sel8", [8, 8, 128], BF16, kind="ExternalInput").ap()
    ccol_in = nc.dram_tensor("ccols", [128, 4], F32, kind="ExternalInput").ap()

    out_d = nc.dram_tensor("out", [T, DM], F32, kind="ExternalOutput").ap()

    w13s = nc.dram_tensor("w13s", [NE, HC, 128, 2, DC, 128], BF16).ap()
    w2s = nc.dram_tensor("w2s", [DC, 128, NE, HC, 128], BF16).ap()
    qs = nc.dram_tensor("qscr", [NT, 128, H, 128], BF16).ap()
    attnscr = nc.dram_tensor("attnscr", [NQ, DC, 128, 512], F32).ap()
    x1s = nc.dram_tensor("x1scr", [T, DM], F32).ap()
    xn2s = nc.dram_tensor("xn2scr", [NQ, 128, DC, 512], BF16).ap()
    rws = nc.dram_tensor("rwscr", [8, T], BF16).ap()

    with ExitStack() as top:
        def sbuf(name, shape, dt, stack):
            return stack.enter_context(nc.sbuf_tensor(name, shape, dt))

        def psum(name, shape, dt, stack):
            return stack.enter_context(nc.psum_tensor(name, shape, dt))

        s_identb = sbuf("s_identb", [128, 128], BF16, top)
        s_identf = sbuf("s_identf", [128, 128], F32, top)
        s_ones1b = sbuf("s_ones1b", [1, 128], BF16, top)
        s_onescb = sbuf("s_onescb", [128, 1], BF16, top)
        s_ccol = sbuf("s_ccol", [128, 4], F32, top)
        s_b13 = sbuf("s_b13", [128, NE, HC, 2], F32, top)
        s_sb2 = sbuf("s_sb2", [128, DC], F32, top)
        s_gwt = sbuf("s_gwt", [128, DC, 8], BF16, top)
        s_gbb = sbuf("s_gbb", [128, 8], F32, top)

        c_epsln = s_ccol[:, 0:1]
        c_epsrms = s_ccol[:, 1:2]
        c_neg1 = s_ccol[:, 2:3]
        c_negbig = s_ccol[:, 3:4]

        cast_sems = {}
        for cname in (["wqkv", "wo", "gate", "b2"] + [f"w13_{e}" for e in range(NE)]
                      + [f"w2_{mc}" for mc in range(DC)]):
            cast_sems[cname] = Ctr(top, nc, "cast_" + cname)

        def cast_wait(eng, cname):
            s = cast_sems[cname]
            eng.wait_ge(s.h, s.n)

        sems = {}
        for name in ["c0",
                     "v1", "s1", "t1",
                     "v2", "s2", "t2",
                     "v25", "s25", "t25",
                     "v3", "s3", "t3",
                     "yLd0", "yLd1", "yQs",
                     "yQq", "yAt0", "yAt1",
                     "yL25a", "yL25b", "yX1", "yXn2",
                     "yW13a", "yW13b", "yW13c", "yW2a", "yW2b", "yW2c",
                     "yXn2f", "yX1t", "yOut", "yRw", "yRwF"]:
            sems[name] = Ctr(top, nc, "sem_" + name)
        c0 = sems["c0"]

        with ExitStack() as p012:
            k_sb = sbuf("k_sb", [128, H, NT, 128], BF16, p012)
            v_sb = sbuf("v_sb", [128, H, NT, 128], BF16, p012)

            v1, s1, t1 = sems["v1"], sems["s1"], sems["t1"]
            yLd = [sems["yLd0"], sems["yLd1"]]
            yQs = sems["yQs"]
            with ExitStack() as p1:
                wq_sb = sbuf("wq_sb", [128, DC, DM], BF16, p1)
                wk_sb = sbuf("wk_sb", [128, DC, DM], BF16, p1)
                wv_sb = sbuf("wv_sb", [128, DC, DM], BF16, p1)
                xb = [sbuf(f"xb{i}", [128, DM], F32, p1) for i in range(2)]
                cb = [sbuf(f"cb{i}", [128, 64], F32, p1) for i in range(2)]
                sb_ = [sbuf(f"sb{i}", [128, 64], F32, p1) for i in range(2)]
                mv = sbuf("mv", [128, 14], F32, p1)
                xn_tm = sbuf("xn_tm", [128, DM], BF16, p1)
                xn_fm = sbuf("xn_fm", [128, DM], BF16, p1)
                sqb = sbuf("sqb", [128, DM], F32, p1)
                qkb_q = sbuf("qkb_q", [128, DM], BF16, p1)
                qkb_k = sbuf("qkb_k", [128, DM], BF16, p1)
                rt_q = sbuf("rt_q", [128, DM], BF16, p1)
                rt_k = sbuf("rt_k", [128, DM], BF16, p1)
                q_rope = sbuf("q_rope", [128, DM], BF16, p1)
                k_rope = sbuf("k_rope", [128, DM], BF16, p1)
                qfm_st = rt_q
                msq = sbuf("msq", [128, 8], F32, p1)
                rsq = sbuf("rsq", [128, 8], F32, p1)
                psQ = psum("psQ", [128, DM], F32, p1)
                psK = psum("psK", [128, DM], F32, p1)
                psV = psum("psV", [128, DM], F32, p1)
                psXP = psum("psXP", [128, DM], BF16, p1)
                psT2 = psum("psT2", [128, DM], BF16, p1)

                ck = {}

                with nc.Block("p1") as blk:
                    @blk.scalar
                    def _(s):
                        s.add_instruction(mybir.InstLoadActFuncSet(
                            name=nc.get_next_instruction_name(),
                            act_func_set_id=LNEXP_SET, ins=[], outs=[]))
                        c0.wait(s, 128)

                    @blk.sync
                    def _(sy):
                        for dst, srcap in [
                            (s_identb[:], identb_in[:]), (s_identf[:], identf_in[:]),
                            (s_ones1b[:], ones1b_in[:]), (s_onescb[:], onescb_in[:]),
                            (s_ccol[:], ccol_in[:]), (s_b13[:], b13_in[:]),
                            (s_sb2[:], sb2_in[:]), (s_gbb[:], gbb_in[:]),
                        ]:
                            c0.inc(sy.dma_start(dst, srcap), 16)

                    @blk.gpsimd
                    def _(g):
                        cq = cast_sems["wqkv"]
                        cq.inc(g.dma_start(wq_sb[:], wq_in.rearrange("(c p) n -> p c n", p=128)), 16)
                        cq.inc(g.dma_start(wk_sb[:], wk_in.rearrange("(c p) n -> p c n", p=128)), 16)
                        cq.inc(g.dma_start(wv_sb[:], wv_in.rearrange("(c p) n -> p c n", p=128)), 16)
                        cast_sems["gate"].inc(g.dma_start(s_gwt[:], gwt_in[:]), 16)
                        for e in range(NE):
                            cast_sems[f"w13_{e}"].inc(g.dma_start(w13s[e], w13_in[e]), 16)
                        for mc in range(DC):
                            cast_sems[f"w2_{mc}"].inc(g.dma_start(w2s[mc], w2_in[mc]), 16)

                    @blk.vector
                    def _(v):
                        c0.wait(v, 128)

                    @blk.tensor
                    def _(tt):
                        c0.wait(tt, 128)

                    def emit_qk_T(t):
                        for which, ro in [("q", q_rope), ("k", k_rope)]:
                            @blk.tensor
                            def _(tt, t=t, which=which, ro=ro):
                                v1.wait(tt, ck[("rope", t, which)])
                                if not (t == 0 and which == "q"):
                                    v1.wait(tt, ck[("psT2",)])
                                for h in range(H):
                                    mm = tt.transpose(psT2[:, h * 128:(h + 1) * 128],
                                                      ro[:, h * 128:(h + 1) * 128], s_identb[:])
                                t1.inc(mm)
                                ck[("qT" if which == "q" else "kT", t)] = t1.n

                            @blk.vector
                            def _(v, t=t, which=which):
                                t1.wait(v)
                                if which == "q":
                                    if t >= 1:
                                        yQs.wait(v, 16 * t)
                                    v1.inc(v.tensor_copy(qfm_st[:], psT2[:]))
                                else:
                                    v1.inc(v.tensor_copy(
                                        k_sb[:, :, t, :],
                                        psT2[:].rearrange("p (h d) -> p h d", h=H)))
                                ck[("psT2",)] = v1.n
                                if which == "q":
                                    ck[("qfm", t)] = v1.n

                        @blk.sync
                        def _(sy, t=t):
                            v1.wait(sy, ck[("qfm", t)])
                            yQs.inc(sy.dma_start(qs[t].rearrange("p h d -> p (h d)"), qfm_st[:]), 16)

                    def emit_loads(t):
                        p = t % 2

                        @blk.sync
                        def _(sy, t=t, p=p):
                            if t >= 2:
                                v1.wait(sy, ck[("ropek", t - 2)])
                                s1.wait(sy, ck[("apply", t - 2)])
                            yLd[p].inc(sy.dma_start(xb[p][:], x_in[t * 128:(t + 1) * 128, :]), 16)
                            yLd[p].inc(sy.dma_start(cb[p][:], cos_in[t]), 16)
                            yLd[p].inc(sy.dma_start(sb_[p][:], sin_in[t]), 16)
                            ck[("xld", t)] = yLd[p].n

                    def emit_ln(t):
                        p = t % 2

                        @blk.vector
                        def _(v, t=t, p=p):
                            yLd[p].wait(v, ck[("xld", t)])
                            v.bn_stats(mv[:, 0:6], xb[p][:, 0:512])
                            v.bn_stats(mv[:, 6:12], xb[p][:, 512:1024])
                            v.drain()
                            v.bn_aggr(mv[:, 12:14], mv[:, 0:12])
                            v.drain()
                            v.tensor_scalar(mv[:, 13:14], mv[:, 13:14], c_epsln,
                                            None, AluOpType.add)
                            v.drain()
                            v1.inc(v.tensor_scalar(mv[:, 11:12], mv[:, 12:13], c_neg1,
                                                   None, AluOpType.mult))
                            ck[("stats", t)] = v1.n

                        @blk.scalar
                        def _(s, t=t, p=p):
                            v1.wait(s, ck[("stats", t)])
                            s.activation(mv[:, 13:14], mv[:, 13:14], AF.Ln)
                            s.drain()
                            s.activation(mv[:, 13:14], mv[:, 13:14], AF.Exp, scale=-0.5)
                            s.drain()
                            s.activation(mv[:, 12:13], mv[:, 11:12], AF.Identity,
                                         scale=mv[:, 13:14])
                            s.drain()
                            if t >= 1:
                                t1.wait(s, ck[("xnT", t - 1)])
                            s1.inc(s.activation(xn_tm[:], xb[p][:], AF.Identity,
                                                bias=mv[:, 12:13], scale=mv[:, 13:14]))
                            ck[("apply", t)] = s1.n

                    emit_loads(0)
                    emit_ln(0)

                    for t in range(NT):
                        p = t % 2

                        if t + 1 < NT:
                            emit_loads(t + 1)

                        @blk.tensor
                        def _(tt, t=t):
                            s1.wait(tt, ck[("apply", t)])
                            if t >= 1:
                                v1.wait(tt, ck[("xnfm", t - 1)])
                            for i in range(8):
                                mmt = tt.transpose(psXP[:, i * 128:(i + 1) * 128],
                                                   xn_tm[:, i * 128:(i + 1) * 128], s_identb[:])
                            t1.inc(mmt)
                            ck[("xnT", t)] = t1.n

                        @blk.vector
                        def _(v, t=t):
                            t1.wait(v, ck[("xnT", t)])
                            if t >= 1:
                                t1.wait(v, ck[("qkv", t - 1)])
                            v1.inc(v.tensor_copy(xn_fm[:], psXP[:]))
                            ck[("xnfm", t)] = v1.n

                        @blk.tensor
                        def _(tt, t=t):
                            if t == 0:
                                cast_wait(tt, "wqkv")
                            v1.wait(tt, ck[("xnfm", t)])
                            if t >= 1:
                                s1.wait(tt, ck[("kscale", t - 1)])
                                v1.wait(tt, ck[("vev", t - 1)])
                            for ps, w in [(psQ, wq_sb), (psK, wk_sb), (psV, wv_sb)]:
                                for hlf in (0, 512):
                                    for dc in range(DC):
                                        mm = tt.matmul(ps[:, hlf:hlf + 512],
                                                       xn_fm[:, dc * 128:(dc + 1) * 128],
                                                       w[:, dc, hlf:hlf + 512],
                                                       start=(dc == 0), stop=(dc == DC - 1))
                            t1.inc(mm)
                            ck[("qkv", t)] = t1.n

                        if t >= 1:
                            emit_qk_T(t - 1)

                        if t + 1 < NT:
                            emit_ln(t + 1)

                        @blk.vector
                        def _(v, t=t):
                            t1.wait(v, ck[("qkv", t)])
                            v1.inc(v.tensor_copy(
                                v_sb[:, :, t, :],
                                psV[:].rearrange("p (h d) -> p h d", h=H)))
                            ck[("vev", t)] = v1.n

                        for which, ps, ro, qkb, rt in [("q", psQ, q_rope, qkb_q, rt_q),
                                                       ("k", psK, k_rope, qkb_k, rt_k)]:
                            @blk.scalar
                            def _(s, t=t, ps=ps, which=which):
                                t1.wait(s, ck[("qkv", t)])
                                if which == "k":
                                    v1.wait(s, ck[("qred", t)])
                                s1.inc(s.activation(sqb[:], ps[:], AF.Square))

                            @blk.vector
                            def _(v, t=t, which=which):
                                s1.wait(v)
                                v1.inc(v.reduce_sum(
                                    msq[:], sqb[:].rearrange("p (h d) -> p h d", h=H), axis=AX))
                                if which == "q":
                                    ck[("qred", t)] = v1.n

                            @blk.scalar
                            def _(s, t=t, ps=ps, which=which, qkb=qkb):
                                v1.wait(s)
                                s.activation(rsq[:], msq[:], AF.Ln, bias=c_epsrms, scale=1.0 / DH)
                                s.drain()
                                s.activation(rsq[:], rsq[:], AF.Exp, scale=-0.5)
                                s.drain()
                                for h in range(H):
                                    mm = s.activation(qkb[:, h * 128:(h + 1) * 128],
                                                      ps[:, h * 128:(h + 1) * 128],
                                                      AF.Identity, scale=rsq[:, h:h + 1])
                                s1.inc(mm)
                                if which == "k":
                                    ck[("kscale", t)] = s1.n

                            @blk.vector
                            def _(v, t=t, p=p, which=which, ro=ro, qkb=qkb, rt=rt):
                                s1.wait(v)
                                if t >= 1:
                                    t1.wait(v, ck[("qT" if which == "q" else "kT", t - 1)])
                                if which == "q" and t >= 1:
                                    yQs.wait(v, 16 * t)
                                r3 = lambda ap: ap.rearrange("p (h d) -> p h d", h=H)
                                ev, od = r3(qkb[:])[:, :, 0:64], r3(qkb[:])[:, :, 64:128]
                                oe, oo = r3(ro[:])[:, :, 0:64], r3(ro[:])[:, :, 64:128]
                                te, to = r3(rt[:])[:, :, 0:64], r3(rt[:])[:, :, 64:128]
                                cosb = bass.AP(cb[p].ap().tensor, cb[p].ap().offset,
                                               [cb[p].ap().ap[0], [0, H], [1, 64]])
                                sinb = bass.AP(sb_[p].ap().tensor, sb_[p].ap().offset,
                                               [sb_[p].ap().ap[0], [0, H], [1, 64]])
                                v.tensor_tensor(te, ev, cosb, AluOpType.mult)
                                v.tensor_tensor(to, od, sinb, AluOpType.mult)
                                v.drain()
                                v.tensor_tensor(oe, te, to, AluOpType.subtract)
                                v.drain()
                                v.tensor_tensor(te, ev, sinb, AluOpType.mult)
                                v.tensor_tensor(to, od, cosb, AluOpType.mult)
                                v.drain()
                                v1.inc(v.tensor_tensor(oo, te, to, AluOpType.add))
                                ck[("rope", t, which)] = v1.n
                                if which == "k":
                                    ck[("ropek", t)] = v1.n

                    emit_qk_T(NT - 1)

            v2, s2, t2 = sems["v2"], sems["s2"], sems["t2"]
            yQq = sems["yQq"]
            yAt = [sems["yAt0"], sems["yAt1"]]
            with ExitStack() as p2:
                s_masks = sbuf("s_masks", [128, 4, 512], BF16, p2)
                s_ones1f = sbuf("s_ones1f", [1, 128], F32, p2)
                wo_sb = sbuf("wo_sb", [128, H, DC, 128], BF16, p2)
                qq = sbuf("qq", [128, 4, H, 128], BF16, p2)
                pt_sl = [sbuf(f"pt{i}", [128, 1024], BF16, p2) for i in range(2)]
                ptm = sbuf("ptm", [128, 4, 512], BF16, p2)
                sden = [sbuf(f"sden{i}", [1, 512], F32, p2) for i in range(2)]
                srden = [sbuf(f"srden{i}", [1, 512], F32, p2) for i in range(2)]
                sdb = [sbuf(f"sdb{i}", [128, 512], F32, p2) for i in range(2)]
                o_all = sbuf("o_all", [128, H, 512], BF16, p2)
                sattn = [sbuf(f"sattn{i}", [128, 512], F32, p2) for i in range(2)]
                psS = [psum(f"psS{i}", [128, 1024], F32, p2) for i in range(2)]
                psO = [psum(f"psO{i}", [128, 512], F32, p2) for i in range(2)]
                psDD = [psum(f"psDD{i}", [128, 512], F32, p2) for i in range(2)]
                psA = psS[0][:, 0:512]

                ck2 = {"exp_last": {}, "pv_last": {}, "onorm_p": {}, "wodone": 0}

                with nc.Block("p2") as blk:
                    @blk.gpsimd
                    def _(g):
                        cast_sems["wo"].inc(g.dma_start(
                            wo_sb[:],
                            bass.AP(wo_in.tensor, 0,
                                    [[1024, 128], [131072, H], [128, DC], [1, 128]])), 16)

                    @blk.scalar
                    def _(s):
                        s.add_instruction(mybir.InstLoadActFuncSet(
                            name=nc.get_next_instruction_name(),
                            act_func_set_id=LNEXP_SET, ins=[], outs=[]))

                    @blk.sync
                    def _(sy):
                        yQs.wait(sy)
                        yLd[0].wait(sy)
                        yLd[1].wait(sy)
                        c0.inc(sy.dma_start(s_masks[:], masks_in[:]), 16)
                        c0.inc(sy.dma_start(s_ones1f[:], ones1f_in[:]), 16)

                    for qg in range(NQ):
                        JC = 4 * (qg + 1)
                        nbatch = (JC + 1) // 2

                        @blk.sync
                        def _(sy, qg=qg):
                            if qg >= 1:
                                t2.wait(sy, ck2[("qq_free", qg - 1)])
                            for i in range(4):
                                yQq.inc(sy.dma_start(qq[:, i, :, :], qs[4 * qg + i]), 16)

                        for h in range(H):
                            hp = h % 2
                            exp_ck, pv_ck = {}, {}
                            batches = [list(range(2 * b, min(2 * b + 2, JC)))
                                       for b in range(nbatch)]

                            def emit_st(b, qg=qg, h=h):
                                jcs = batches[b]

                                @blk.tensor
                                def _(tt):
                                    if b == 0 and h == 0:
                                        yQq.wait(tt)
                                    if (b - 2) in exp_ck:
                                        s2.wait(tt, exp_ck[b - 2])
                                    elif b < 2 and (b % 2) in ck2["exp_last"]:
                                        s2.wait(tt, ck2["exp_last"][b % 2])
                                    for i, jc in enumerate(jcs):
                                        mm = tt.matmul(psS[b % 2][:, i * 512:(i + 1) * 512],
                                                       k_sb[:, h, jc, :], qq[:, :, h, :],
                                                       start=True, stop=True)
                                    t2.inc(mm)
                                    ck2[("st", b)] = t2.n

                            def emit_exp(b, qg=qg, h=h):
                                jcs = batches[b]

                                @blk.scalar
                                def _(s):
                                    t2.wait(s, ck2[("st", b)])
                                    if (b - 2) in pv_ck:
                                        t2.wait(s, pv_ck[b - 2])
                                    elif b < 2 and (b % 2) in ck2["pv_last"]:
                                        t2.wait(s, ck2["pv_last"][b % 2])
                                    s2.inc(s.activation(pt_sl[b % 2][:, 0:len(jcs) * 512],
                                                        psS[b % 2][:, 0:len(jcs) * 512],
                                                        AF.Exp, scale=1.0 / DH))
                                    exp_ck[b] = s2.n
                                    ck2["exp_last"][b % 2] = s2.n

                                diag = [i for i, jc in enumerate(jcs) if jc >= 4 * qg]
                                if diag:
                                    @blk.vector
                                    def _(v):
                                        if qg == 0 and h == 0:
                                            c0.wait(v)
                                        s2.wait(v, exp_ck[b])
                                        for i in diag:
                                            kk = jcs[i] - 4 * qg
                                            mm = v.tensor_tensor(
                                                ptm[:, kk, :],
                                                pt_sl[b % 2][:, i * 512:(i + 1) * 512],
                                                s_masks[:, kk, :], AluOpType.mult)
                                        v2.inc(mm)
                                        ck2[("mask", b)] = v2.n

                            def emit_pv(b, qg=qg, h=h, hp=hp):
                                jcs = batches[b]

                                @blk.tensor
                                def _(tt):
                                    s2.wait(tt, exp_ck[b])
                                    if any(jc >= 4 * qg for jc in jcs):
                                        v2.wait(tt, ck2[("mask", b)])
                                    if b == 0 and (hp in ck2["onorm_p"]):
                                        v2.wait(tt, ck2["onorm_p"][hp])
                                    for i, jc in enumerate(jcs):
                                        src = (ptm[:, jc - 4 * qg, :] if jc >= 4 * qg
                                               else pt_sl[b % 2][:, i * 512:(i + 1) * 512])
                                        tt.matmul(psO[hp][:], v_sb[:, h, jc, :], src,
                                                  start=(jc == 0), stop=(jc == JC - 1))
                                        mm = tt.matmul(psDD[hp][0:1, :], s_onescb[:], src,
                                                       start=(jc == 0), stop=(jc == JC - 1))
                                    t2.inc(mm)
                                    pv_ck[b] = t2.n
                                    ck2["pv_last"][b % 2] = t2.n

                            emit_st(0)
                            emit_exp(0)
                            for b in range(1, nbatch):
                                emit_st(b)
                                emit_pv(b - 1)
                                emit_exp(b)
                            emit_pv(nbatch - 1)

                            @blk.vector
                            def _(v, qg=qg, h=h, hp=hp, pv_ck=pv_ck, nbatch=nbatch):
                                t2.wait(v, pv_ck[nbatch - 1])
                                if qg >= 1 and h == 0:
                                    t2.wait(v, ck2["wodone"])
                                v.tensor_copy(sden[hp][:], psDD[hp][0:1, :])
                                v.drain()
                                v2.inc(v.reciprocal(srden[hp][:], sden[hp][:]))

                            @blk.tensor
                            def _(tt, qg=qg, h=h, hp=hp):
                                if qg == 0 and h == 0:
                                    c0.wait(tt)
                                v2.wait(tt)
                                t2.inc(tt.matmul(psDD[hp][:], s_ones1f[:], srden[hp][:],
                                                 start=True, stop=True))

                            @blk.vector
                            def _(v, qg=qg, h=h, hp=hp):
                                t2.wait(v)
                                v.tensor_copy(sdb[hp][:], psDD[hp][:])
                                v.drain()
                                v2.inc(v.tensor_tensor(o_all[:, h, :], psO[hp][:], sdb[hp][:],
                                                       AluOpType.mult))
                                ck2["onorm_p"][hp] = v2.n
                                if h == H - 1:
                                    ck2[("heads_done", qg)] = v2.n

                        ck2[("qq_free", qg)] = t2.n

                        for mc in range(DC):
                            @blk.tensor
                            def _(tt, qg=qg, mc=mc):
                                if qg == 0 and mc == 0:
                                    cast_wait(tt, "wo")
                                v2.wait(tt, ck2[("heads_done", qg)])
                                if ("attn_ev", qg, mc - 1) in ck2:
                                    v2.wait(tt, ck2[("attn_ev", qg, mc - 1)])
                                elif qg >= 1 and mc == 0:
                                    v2.wait(tt, ck2[("attn_ev", qg - 1, DC - 1)])
                                for h in range(H):
                                    mm = tt.matmul(psA, wo_sb[:, h, mc, :], o_all[:, h, :],
                                                   start=(h == 0), stop=(h == H - 1))
                                t2.inc(mm)
                                if mc == DC - 1:
                                    ck2["wodone"] = t2.n

                            @blk.vector
                            def _(v, qg=qg, mc=mc):
                                t2.wait(v)
                                k_st = qg * DC + mc
                                if k_st >= 2:
                                    yAt[mc % 2].wait(v, 16 * (k_st // 2))
                                v2.inc(v.tensor_copy(sattn[mc % 2][:], psA))
                                ck2[("attn_ev", qg, mc)] = v2.n

                            @blk.sync
                            def _(sy, qg=qg, mc=mc):
                                v2.wait(sy, ck2[("attn_ev", qg, mc)])
                                yAt[mc % 2].inc(sy.dma_start(attnscr[qg, mc], sattn[mc % 2][:]), 16)

            v25, s25, t25 = sems["v25"], sems["s25"], sems["t25"]
            yL25 = [sems["yL25a"], sems["yL25b"]]
            yX1 = sems["yX1"]
            yXn2 = sems["yXn2"]
            yRw = sems["yRw"]
            with ExitStack() as p25:
                xb2 = [sbuf(f"xb2_{i}", [128, DM], F32, p25) for i in range(2)]
                afm = [sbuf(f"afm{i}", [128, DC, 128], F32, p25) for i in range(2)]
                x1tm = sbuf("x1tm", [128, DM], F32, p25)
                xn2tm = sbuf("xn2tm", [128, DM], BF16, p25)
                xn2st = [sbuf(f"xn2st{i}", [128, DC, 512], BF16, p25) for i in range(2)]
                mv2 = sbuf("mv2", [128, 14], F32, p25)
                sxp = sbuf("sxp", [128, 8], F32, p25)
                sume = sbuf("sume", [128, 1], F32, p25)
                rsum = sbuf("rsum", [128, 1], F32, p25)
                rsc = sbuf("rsc", [128, 8], F32, p25)
                bsd = sbuf("bsd", [128, 8], F32, p25)
                m1c = sbuf("m1c", [128, 1], F32, p25)
                mask1 = sbuf("mask1", [128, 8], F32, p25)
                b2d = sbuf("b2d", [128, 8], F32, p25)
                m2c = sbuf("m2c", [128, 1], F32, p25)
                mask2 = sbuf("mask2", [128, 8], F32, p25)
                rwbf = sbuf("rwbf", [128, 8], BF16, p25)
                rw_st = sbuf("rw_st", [8, 128], BF16, p25)
                psT25 = psum("psT25", [128, DM], F32, p25)
                psX2 = psum("psX2", [128, DM], BF16, p25)
                psG = psum("psG", [128, 512], F32, p25)
                psRw = psum("psRw", [8, 512], BF16, p25)

                ck25 = {}

                with nc.Block("p25") as blk:
                    @blk.sync
                    def _(sy):
                        yAt[0].wait(sy)
                        yAt[1].wait(sy)
                        yQq.wait(sy)

                    for t in range(NT):
                        p = t % 2
                        qg, off = t // 4, (t % 4) * 128

                        @blk.sync
                        def _(sy, t=t, p=p, qg=qg, off=off):
                            if t >= 2:
                                v25.wait(sy, ck25[("vfin", t - 2)])
                            yL25[p].inc(sy.dma_start(xb2[p][:], x_in[t * 128:(t + 1) * 128, :]), 16)
                            yL25[p].inc(sy.dma_start(
                                afm[p][:],
                                bass.AP(attnscr.tensor, qg * DC * 128 * 512 + off,
                                        [[512, 128], [128 * 512, DC], [1, 128]])), 16)
                            ck25[("ld", t)] = yL25[p].n

                        @blk.tensor
                        def _(tt, t=t, p=p):
                            yL25[p].wait(tt, ck25[("ld", t)])
                            if t >= 1:
                                v25.wait(tt, ck25[("x1add", t - 1)])
                            for mc in range(DC):
                                mm = tt.transpose(psT25[:, mc * 128:(mc + 1) * 128],
                                                  afm[p][:, mc, :], s_identf[:])
                            t25.inc(mm)

                        @blk.vector
                        def _(v, t=t, p=p):
                            t25.wait(v)
                            if t >= 1:
                                yX1.wait(v, 16 * t)
                            v25.inc(v.tensor_tensor(x1tm[:], psT25[:], xb2[p][:], AluOpType.add))
                            ck25[("x1add", t)] = v25.n

                        @blk.sync
                        def _(sy, t=t):
                            v25.wait(sy, ck25[("x1add", t)])
                            yX1.inc(sy.dma_start(x1s[t * 128:(t + 1) * 128, :], x1tm[:]), 16)

                        @blk.vector
                        def _(v, t=t):
                            v.drain()
                            v.bn_stats(mv2[:, 0:6], x1tm[:, 0:512])
                            v.bn_stats(mv2[:, 6:12], x1tm[:, 512:1024])
                            v.drain()
                            v.bn_aggr(mv2[:, 12:14], mv2[:, 0:12])
                            v.drain()
                            v.tensor_scalar(mv2[:, 13:14], mv2[:, 13:14],
                                            c_epsln, None, AluOpType.add)
                            v.drain()
                            v25.inc(v.tensor_scalar(mv2[:, 11:12], mv2[:, 12:13], c_neg1,
                                                    None, AluOpType.mult))

                        @blk.scalar
                        def _(s, t=t):
                            v25.wait(s)
                            s.activation(mv2[:, 13:14], mv2[:, 13:14], AF.Ln)
                            s.drain()
                            s.activation(mv2[:, 13:14], mv2[:, 13:14], AF.Exp, scale=-0.5)
                            s.drain()
                            s.activation(mv2[:, 12:13], mv2[:, 11:12], AF.Identity,
                                         scale=mv2[:, 13:14])
                            s.drain()
                            if t >= 1:
                                t25.wait(s, ck25[("x2t", t - 1)])
                            s25.inc(s.activation(xn2tm[:], x1tm[:], AF.Identity,
                                                 bias=mv2[:, 12:13], scale=mv2[:, 13:14]))
                            ck25[("apply2", t)] = s25.n

                        @blk.tensor
                        def _(tt, t=t):
                            s25.wait(tt, ck25[("apply2", t)])
                            if t >= 1:
                                v25.wait(tt, ck25[("x2ev", t - 1)])
                            for i in range(DC):
                                mm = tt.transpose(psX2[:, i * 128:(i + 1) * 128],
                                                  xn2tm[:, i * 128:(i + 1) * 128], s_identb[:])
                            t25.inc(mm)
                            ck25[("x2t", t)] = t25.n

                        @blk.vector
                        def _(v, t=t, qg=qg, off=off):
                            t25.wait(v, ck25[("x2t", t)])
                            if t % 4 == 0 and qg >= 2:
                                yXn2.wait(v, 16 * qg)
                                t25.wait(v, ck25[("gmm", 4 * (qg - 2) + 3)])
                            v25.inc(v.tensor_copy(
                                xn2st[qg % 2][:, :, off:off + 128],
                                psX2[:].rearrange("p (c d) -> p c d", c=DC)))
                            ck25[("x2ev", t)] = v25.n

                        @blk.sync
                        def _(sy, t=t, qg=qg):
                            if t % 4 == 3:
                                v25.wait(sy, ck25[("x2ev", t)])
                                yXn2.inc(sy.dma_start(xn2s[qg], xn2st[qg % 2][:]), 16)

                        @blk.tensor
                        def _(tt, t=t, qg=qg, off=off):
                            if t == 0:
                                cast_wait(tt, "gate")
                            v25.wait(tt, ck25[("x2ev", t)])
                            if t >= 1:
                                s25.wait(tt, ck25[("gexp", t - 1)])
                            for dc in range(DC):
                                mm = tt.matmul(psG[:, 0:8],
                                               xn2st[qg % 2][:, dc, off:off + 128],
                                               s_gwt[:, dc, :],
                                               start=(dc == 0), stop=(dc == DC - 1))
                            t25.inc(mm)
                            ck25[("gmm", t)] = t25.n

                        @blk.vector
                        def _(v, t=t):
                            t25.wait(v, ck25[("gmm", t)])
                            v25.inc(v.reduce_max(m1c[:], psG[:, 0:8], axis=AX, negate=True))

                        @blk.scalar
                        def _(s, t=t):
                            v25.wait(s)
                            s25.inc(s.activation(sxp[:], psG[:, 0:8], AF.Exp,
                                                 bias=m1c[:], accum_out=sume[:]))
                            ck25[("gexp", t)] = s25.n

                        @blk.vector
                        def _(v, t=t):
                            s25.wait(v, ck25[("gexp", t)])
                            v.reciprocal(rsum[:], sume[:])
                            v.drain()
                            v.tensor_scalar(rsc[:], sxp[:], rsum[:], None, AluOpType.mult)
                            v.drain()
                            v.tensor_tensor(bsd[:], rsc[:], s_gbb[:], AluOpType.add)
                            v.drain()
                            v.reduce_max(m1c[:], bsd[:], axis=AX)
                            v.drain()
                            v.tensor_scalar(mask1[:], bsd[:], m1c[:], None, AluOpType.is_ge)
                            v.drain()
                            v.scalar_tensor_tensor(b2d[:], mask1[:], c_negbig, bsd[:],
                                                   AluOpType.mult, AluOpType.add)
                            v.drain()
                            v.reduce_max(m2c[:], b2d[:], axis=AX)
                            v.drain()
                            v.tensor_scalar(mask2[:], b2d[:], m2c[:], None, AluOpType.is_ge)
                            v.drain()
                            v.tensor_tensor(mask1[:], mask1[:], mask2[:], AluOpType.add)
                            v.drain()
                            v25.inc(v.tensor_tensor(rwbf[:], rsc[:], mask1[:], AluOpType.mult))

                        @blk.tensor
                        def _(tt, t=t):
                            v25.wait(tt)
                            if t >= 1:
                                v25.wait(tt, ck25[("rwev", t - 1)])
                            t25.inc(tt.transpose(psRw[:, 0:128], rwbf[:], s_identb[:]))

                        @blk.vector
                        def _(v, t=t):
                            t25.wait(v)
                            if t >= 1:
                                yRw.wait(v, 16 * t)
                            v25.inc(v.tensor_copy(rw_st[:], psRw[:, 0:128]))
                            ck25[("rwev", t)] = v25.n

                        @blk.sync
                        def _(sy, t=t):
                            v25.wait(sy, ck25[("rwev", t)])
                            yRw.inc(sy.dma_start(
                                bass.AP(rws.tensor, t * 128, [[T, 8], [1, 128]]),
                                rw_st[:]), 16)

                        ck25[("vfin", t)] = v25.n

        v3, s3, t3 = sems["v3"], sems["s3"], sems["t3"]
        yRw = sems["yRw"]
        yRwF = sems["yRwF"]
        yW13 = [sems["yW13a"], sems["yW13b"], sems["yW13c"]]
        yW2 = [sems["yW2a"], sems["yW2b"], sems["yW2c"]]
        yXn2f = sems["yXn2f"]
        yX1t = sems["yX1t"]
        yOut = sems["yOut"]
        with ExitStack() as p3:
            h_all = sbuf("h_all", [128, NE, HC, 512], BF16, p3)
            s_b2 = sbuf("s_b2", [8, DM], BF16, p3)
            s_sel8 = sbuf("s_sel8", [8, 8, 128], BF16, p3)
            w13b = [sbuf(f"w13b{i}", [128, 2 * 2048], BF16, p3) for i in range(3)]
            w2b = [sbuf(f"w2b{i}", [128, 3 * HC * 128], BF16, p3) for i in range(3)]
            xn2f = sbuf("xn2f", [128, DC, 512], BF16, p3)
            x1t = sbuf("x1t", [128, 4, DM], F32, p3)
            rwT_f = sbuf("rwT_f", [8, 512], BF16, p3)
            rwb = [sbuf(f"rwb{i}", [128, 512], BF16, p3) for i in range(2)]
            sA = [sbuf(f"sA{i}", [128, 512], F32, p3) for i in range(2)]
            sProd = sbuf("sProd", [128, 4, 512], F32, p3)
            htmp4 = sbuf("htmp4", [128, 2048], BF16, p3)
            yf = [sbuf(f"yf{i}", [128, 512], F32, p3) for i in range(2)]
            ost = [sbuf(f"ost{i}", [128, 4, 128], F32, p3) for i in range(1)]
            psA3 = [psum(f"psA3{i}", [128, 512], F32, p3) for i in range(2)]
            psB3 = [psum(f"psB3{i}", [128, 512], F32, p3) for i in range(2)]
            psY = [psum(f"psY{i}", [128, 512], F32, p3) for i in range(2)]
            psYT = psum("psYT", [128, 512], F32, p3)
            psR = psum("psR", [128, 512], F32, p3)

            NSL = len(W13_SL)
            n_sl = NQ * NE * NSL

            def sl_idx(f, e, si):
                return (f * NE + e) * NSL + si

            def sl_of(k):
                f, r = divmod(k, NE * NSL)
                e, si = divmod(r, NSL)
                return f, e, si

            def w13_ap(e, si):
                h0, h1 = W13_SL[si]
                base = (e * HC + h0) * (128 * 2048)
                return bass.AP(w13s.tensor, base,
                               [[2048, 128], [128 * 2048, h1 - h0], [1, 2048]])

            def w2_ap(mc, gi):
                e0, e1 = W2_GR[gi]
                base = mc * (128 * NE * HC * 128) + e0 * (HC * 128)
                return bass.AP(w2s.tensor, base,
                               [[NE * HC * 128, 128], [1, (e1 - e0) * HC * 128]])

            ck3 = {}

            with nc.Block("p3") as blk:
                @blk.gpsimd
                def _(g):
                    cast_sems["b2"].inc(g.dma_start(s_b2[:], b2_in[:]), 16)

                @blk.scalar
                def _(s):
                    s.add_instruction(mybir.InstLoadActFuncSet(
                        name=nc.get_next_instruction_name(),
                        act_func_set_id=SILU_SET, ins=[], outs=[]))

                @blk.sync
                def _(sy):
                    yX1.wait(sy)
                    yXn2.wait(sy)
                    yRw.wait(sy)
                    yL25[0].wait(sy)
                    yL25[1].wait(sy)
                    c0.inc(sy.dma_start(s_sel8[:], sel8_in[:]), 16)
                    for k in range(3):
                        f_, e_, si_ = sl_of(k)
                        cast_wait(sy, f"w13_{e_}")
                        hn = W13_SL[si_][1] - W13_SL[si_][0]
                        yW13[k % 3].inc(sy.dma_start(w13b[k % 3][:, 0:hn * 2048],
                                                     w13_ap(e_, si_)), 16)
                        ck3[("w13ld", k)] = yW13[k % 3].n

                w2q = [(f, mc, gi) for f in range(NQ) for mc in range(DC) for gi in range(3)]

                @blk.scalar
                def _(s):
                    yX1.wait(s)
                    yXn2.wait(s)
                    yL25[0].wait(s)
                    yL25[1].wait(s)
                    for gq in range(3):
                        f_, mc_, gi_ = w2q[gq]
                        cast_wait(s, f"w2_{mc_}")
                        yW2[gq % 3].inc(s.dma_start(w2b[gq % 3][:], w2_ap(mc_, gi_)), 16)
                        ck3[("w2ld", gq)] = yW2[gq % 3].n

                for f in range(NQ):
                    @blk.sync
                    def _(sy, f=f):
                        if f >= 1:
                            t3.wait(sy, ck3[("ymm", f - 1, DC - 1)])
                        yRwF.inc(sy.dma_start(
                            rwT_f[:], bass.AP(rws.tensor, f * 512, [[T, 8], [1, 512]])), 16)
                        if f >= 1:
                            t3.wait(sy, ck3[("sAdone", f - 1)])
                        yXn2f.inc(sy.dma_start(xn2f[:], xn2s[f]), 16)
                        if f >= 1:
                            v3.wait(sy, ck3[("ost", f - 1)])
                        yX1t.inc(sy.dma_start(
                            x1t[:],
                            bass.AP(x1s.tensor, f * 512 * DM,
                                    [[DM, 128], [128 * DM, 4], [1, DM]])), 16)

                    first_s = [True]
                    first_v = [True]
                    for e in range(NE):
                        if e < 8:
                            @blk.tensor
                            def _(tt, f=f, e=e):
                                if f == 0 and e == 0:
                                    yRwF.wait(tt, 16)
                                elif e == 0:
                                    yRwF.wait(tt, 16 * (f + 1))
                                if ("rwev", e - 1) in ck3:
                                    v3.wait(tt, ck3[("rwev", e - 1)])
                                elif f >= 1 and e == 0:
                                    v3.wait(tt, ck3[("rwev", 7)])
                                if f == 0 and e == 0:
                                    c0.wait(tt)
                                t3.inc(tt.matmul(psR[:], s_sel8[:, e, :], rwT_f[:],
                                                 start=True, stop=True))

                            @blk.vector
                            def _(v, f=f, e=e):
                                t3.wait(v)
                                v.drain()
                                v.tensor_copy(rwb[e % 2][:], psR[:])
                                v3.inc(v.drain())
                                ck3[("rwev", e)] = v3.n

                        for si in range(NSL):
                            k = sl_idx(f, e, si)
                            h0, h1 = W13_SL[si]
                            for hc in range(h0, h1):
                                hp = hc % 2
                                slot = hc % 4
                                grp = hc // 4
                                g0_, g1_ = grp * 4, min(grp * 4 + 4, HC)

                                @blk.tensor
                                def _(tt, f=f, e=e, si=si, hc=hc, hp=hp, k=k, h0=h0):
                                    if hc == h0:
                                        yW13[k % 3].wait(tt, ck3[("w13ld", k)])
                                        if e == 0 and si == 0:
                                            yXn2f.wait(tt, 16 * (f + 1))
                                    if ("ab_ev", hp) in ck3:
                                        v3.wait(tt, ck3[("ab_ev", hp)])
                                    base = (hc - h0) * 2048
                                    for dc in range(DC):
                                        tt.matmul(psA3[hp][:],
                                                  w13b[k % 3][:, base + dc * 128:base + dc * 128 + 128],
                                                  xn2f[:, dc, :],
                                                  start=(dc == 0), stop=(dc == DC - 1))
                                    for dc in range(DC):
                                        mm = tt.matmul(psB3[hp][:],
                                                       w13b[k % 3][:, base + 1024 + dc * 128:base + 1024 + dc * 128 + 128],
                                                       xn2f[:, dc, :],
                                                       start=(dc == 0), stop=(dc == DC - 1))
                                    t3.inc(mm)
                                    ck3[("ab", hp)] = t3.n
                                    if hc == h1 - 1:
                                        ck3[("w13use", k)] = t3.n

                                @blk.vector
                                def _(v, f=f, e=e, hc=hc, hp=hp, slot=slot, grp=grp):
                                    t3.wait(v, ck3[("ab", hp)])
                                    if first_v[0]:
                                        first_v[0] = False
                                        if f >= 1:
                                            t3.wait(v, ck3[("ydone", f - 1)])
                                    if ("silu_g", e, grp - 1) in ck3:
                                        s3.wait(v, ck3[("silu_g", e, grp - 1)])
                                    elif grp == 0 and ("silu_last",) in ck3:
                                        s3.wait(v, ck3[("silu_last",)])
                                    v.tensor_scalar(sA[hp][:], psA3[hp][:],
                                                    s_b13[:, e, hc, 0:1], None, AluOpType.add)
                                    v.drain()
                                    v3.inc(v.scalar_tensor_tensor(
                                        sProd[:, slot, :], psB3[hp][:], s_b13[:, e, hc, 1:2],
                                        sA[hp][:], AluOpType.add, AluOpType.mult))
                                    ck3[("ab_ev", hp)] = v3.n
                                    ck3[("pr", e, hc)] = v3.n

                                if hc == g1_ - 1:
                                    ng = g1_ - g0_

                                    @blk.scalar
                                    def _(s, f=f, e=e, grp=grp, g0_=g0_, g1_=g1_, ng=ng):
                                        v3.wait(s, ck3[("pr", e, g1_ - 1)])
                                        if first_s[0]:
                                            first_s[0] = False
                                            if f >= 1:
                                                t3.wait(s, ck3[("ydone", f - 1)])
                                        if ("hsc_last",) in ck3:
                                            v3.wait(s, ck3[("hsc_last",)])
                                        if e == 8:
                                            mm = s.activation(
                                                h_all[:, e, g0_:g1_, :].rearrange("p g d -> p (g d)"),
                                                sProd[:, 0:ng, :].rearrange("p g d -> p (g d)"),
                                                AF.Silu)
                                        else:
                                            mm = s.activation(
                                                htmp4[:, 0:ng * 512],
                                                sProd[:, 0:ng, :].rearrange("p g d -> p (g d)"),
                                                AF.Silu)
                                        s3.inc(mm)
                                        ck3[("silu_g", e, grp)] = s3.n
                                        ck3[("silu_last",)] = s3.n

                                    if e < 8:
                                        @blk.vector
                                        def _(v, f=f, e=e, grp=grp, g0_=g0_, g1_=g1_, ng=ng):
                                            s3.wait(v, ck3[("silu_g", e, grp)])
                                            v3.wait(v, ck3[("rwev", e)])
                                            rwb_b = bass.AP(
                                                rwb[e % 2].ap().tensor,
                                                rwb[e % 2].ap().offset,
                                                [rwb[e % 2].ap().ap[0], [0, ng], [1, 512]])
                                            v3.inc(v.tensor_tensor(
                                                h_all[:, e, g0_:g1_, :],
                                                htmp4[:, 0:ng * 512].rearrange(
                                                    "p (g d) -> p g d", g=ng),
                                                rwb_b, AluOpType.mult))
                                            ck3[("hsc_last",)] = v3.n

                            if k + 3 < n_sl:
                                f2, e2, si2 = sl_of(k + 3)

                                @blk.sync
                                def _(sy, k=k, f2=f2, e2=e2, si2=si2):
                                    t3.wait(sy, ck3[("w13use", k)])
                                    cast_wait(sy, f"w13_{e2}")
                                    hn = W13_SL[si2][1] - W13_SL[si2][0]
                                    yW13[k % 3].inc(sy.dma_start(
                                        w13b[(k + 3) % 3][:, 0:hn * 2048],
                                        w13_ap(e2, si2)), 16)
                                    ck3[("w13ld", k + 3)] = yW13[k % 3].n

                    ck3[("sAdone", f)] = t3.n
                    ck3[("hready", f)] = (s3.n, v3.n)

                    for mc in range(DC):
                        qpos = (f * DC + mc) * 3
                        for gi, (e0, e1) in enumerate(W2_GR):
                            gq = qpos + gi

                            @blk.tensor
                            def _(tt, f=f, mc=mc, gi=gi, e0=e0, e1=e1, gq=gq):
                                if gi == 0:
                                    sck, vck = ck3[("hready", f)]
                                    s3.wait(tt, sck)
                                    v3.wait(tt, vck)
                                    if f == 0 and mc == 0:
                                        cast_wait(tt, "b2")
                                    if ("yevac", f, mc - 2) in ck3:
                                        s3.wait(tt, ck3[("yevac", f, mc - 2)])
                                    elif f >= 1 and mc < 2:
                                        s3.wait(tt, ck3[("yevac", f - 1, DC - 2 + mc)])
                                yW2[gq % 3].wait(tt, ck3[("w2ld", gq)])
                                for e in range(e0, e1):
                                    for hc in range(HC):
                                        mm = tt.matmul(
                                            psY[mc % 2][:],
                                            w2b[gq % 3][:, ((e - e0) * HC + hc) * 128:
                                                        ((e - e0) * HC + hc) * 128 + 128],
                                            h_all[:, e, hc, :],
                                            start=(gi == 0 and e == e0 and hc == 0), stop=False)
                                if gi == 2:
                                    mm = tt.matmul(psY[mc % 2][:],
                                                   s_b2[:, mc * 128:(mc + 1) * 128],
                                                   rwT_f[:],
                                                   start=False, stop=True)
                                t3.inc(mm)
                                ck3[("w2use", gq)] = t3.n
                                if gi == 2:
                                    ck3[("ymm", f, mc)] = t3.n
                                    if mc == DC - 1:
                                        ck3[("ydone", f)] = t3.n

                            if gq + 3 < len(w2q):
                                f2, mc2, g2 = w2q[gq + 3]

                                @blk.scalar
                                def _(s, gq=gq, f2=f2, mc2=mc2, g2=g2):
                                    t3.wait(s, ck3[("w2use", gq)])
                                    cast_wait(s, f"w2_{mc2}")
                                    yW2[gq % 3].inc(s.dma_start(w2b[(gq + 3) % 3][:],
                                                                w2_ap(mc2, g2)), 16)
                                    ck3[("w2ld", gq + 3)] = yW2[gq % 3].n

                        @blk.scalar
                        def _(s, f=f, mc=mc):
                            t3.wait(s, ck3[("ymm", f, mc)])
                            s3.inc(s.activation(yf[mc % 2][:], psY[mc % 2][:], AF.Identity,
                                                bias=s_sb2[:, mc:mc + 1]))
                            ck3[("yevac", f, mc)] = s3.n

                        @blk.tensor
                        def _(tt, f=f, mc=mc):
                            s3.wait(tt, ck3[("yevac", f, mc)])
                            if ("ytev", f, mc - 1) in ck3:
                                v3.wait(tt, ck3[("ytev", f, mc - 1)])
                            elif f >= 1 and mc == 0:
                                v3.wait(tt, ck3[("ytev", f - 1, DC - 1)])
                            for sub in range(4):
                                mm = tt.transpose(psYT[:, sub * 128:(sub + 1) * 128],
                                                  yf[mc % 2][:, sub * 128:(sub + 1) * 128],
                                                  s_identf[:])
                            t3.inc(mm)
                            ck3[("yt", f, mc)] = t3.n

                        @blk.vector
                        def _(v, f=f, mc=mc):
                            t3.wait(v, ck3[("yt", f, mc)])
                            yX1t.wait(v, 16 * (f + 1))
                            k_out = f * DC + mc
                            if k_out >= 1:
                                yOut.wait(v, 64 * k_out)
                            v3.inc(v.tensor_tensor(
                                ost[0][:],
                                psYT[:].rearrange("p (s d) -> p s d", s=4),
                                bass.AP(x1t.ap().tensor, x1t.ap().offset + mc * 128,
                                        [x1t.ap().ap[0], [DM, 4], [1, 128]]),
                                AluOpType.add))
                            ck3[("ytev", f, mc)] = v3.n
                            if mc == DC - 1:
                                ck3[("ost", f)] = v3.n

                        @blk.sync
                        def _(sy, f=f, mc=mc):
                            v3.wait(sy, ck3[("ytev", f, mc)])
                            for sub in range(4):
                                yOut.inc(sy.dma_start(
                                    out_d[f * 512 + sub * 128:f * 512 + (sub + 1) * 128,
                                          mc * 128:(mc + 1) * 128],
                                    ost[0][:, sub, :]), 16)

                @blk.sync
                def _(sy):
                    yOut.wait(sy)

    return nc



def _prep_inputs(T, ins):
    NT = T // 128
    f32 = np.float32
    w1, w3, w2 = ins["L0_ffn_w1"], ins["L0_ffn_w3"], ins["L0_ffn_w2"]
    sw1, sw3, sw2 = ins["L0_ffn_sw1"], ins["L0_ffn_sw3"], ins["L0_ffn_sw2"]
    b1, b3, b2 = ins["L0_ffn_b1"], ins["L0_ffn_b3"], ins["L0_ffn_b2"]
    sb1, sb3, sb2 = ins["L0_ffn_sb1"], ins["L0_ffn_sb3"], ins["L0_ffn_sb2"]

    w1a = np.concatenate([w1, sw1[None]], 0)
    w3a = np.concatenate([w3, sw3[None]], 0)
    w13 = np.stack([w1a, w3a], 1)
    w13 = w13.reshape(NE, 2, DC, 128, HC, 128)
    w13 = np.ascontiguousarray(w13.transpose(0, 4, 3, 1, 2, 5))

    w2a = np.concatenate([w2, sw2[None]], 0)
    w2a = w2a.reshape(NE, HC, 128, DC, 128)
    w2a = np.ascontiguousarray(w2a.transpose(3, 2, 0, 1, 4))

    b1a = np.concatenate([b1, sb1[None]], 0)
    b3a = np.concatenate([b3, sb3[None]], 0)
    b13 = np.stack([b1a, b3a], -1).reshape(NE, HC, 128, 2)
    b13 = np.ascontiguousarray(b13.transpose(2, 0, 1, 3))

    gate_w = ins["L0_ffn_gate_w"]
    gwt = np.ascontiguousarray(gate_w.T.reshape(DC, 128, 8).transpose(1, 0, 2))
    gbb = np.ascontiguousarray(
        np.broadcast_to(np.asarray(ins["L0_ffn_gate_b"], f32)[None, :], (128, 8)))

    pos = np.arange(T, dtype=np.float64)
    freqs = np.power(10000.0, -np.arange(64, dtype=np.float64) / 64.0)
    rad = pos[:, None] * freqs[None, :]
    cosT = np.cos(rad).astype(f32).reshape(NT, 128, 64)
    sinT = np.sin(rad).astype(f32).reshape(NT, 128, 64)

    j = np.arange(128)[:, None]
    qq = np.arange(512)[None, :]
    masks = np.stack([(qq >= j + 128 * k) for k in range(4)], 0).astype(ml_dtypes.bfloat16)
    masks = np.ascontiguousarray(masks.transpose(1, 0, 2))

    ccols = np.zeros((128, 4), f32)
    ccols[:, 0] = EPS_LN
    ccols[:, 1] = EPS_RMS
    ccols[:, 2] = -1.0
    ccols[:, 3] = NEG_BIG

    return {
        "wq": np.ascontiguousarray(ins["L0_attn_wq"].reshape(DM, DM)),
        "wk": np.ascontiguousarray(ins["L0_attn_wk"].reshape(DM, DM)),
        "wv": np.ascontiguousarray(ins["L0_attn_wv"].reshape(DM, DM)),
        "wo": np.ascontiguousarray(ins["L0_attn_wo"]),
        "w13": w13, "w2": w2a, "b13": b13,
        "b2": np.ascontiguousarray(b2),
        "sb2c": np.ascontiguousarray(np.asarray(sb2, f32).reshape(DC, 128).T),
        "gwt": gwt, "gbb": gbb,
        "cosT": cosT, "sinT": sinT,
        "identb": np.eye(128, dtype=ml_dtypes.bfloat16),
        "identf": np.eye(128, dtype=f32),
        "ones1b": np.ones((1, 128), ml_dtypes.bfloat16),
        "onescb": np.ones((128, 1), ml_dtypes.bfloat16),
        "ones1f": np.ones((1, 128), f32),
        "masksb": masks,
        "ccols": ccols,
        "sel8": np.ascontiguousarray(
            np.eye(8, dtype=ml_dtypes.bfloat16)[:, :, None]
            * np.ones((1, 1, 128), ml_dtypes.bfloat16)),
    }


def kernel(**ins):
    x = np.asarray(ins["L0_attn_x"])
    B, T, _ = x.shape
    base = _prep_inputs(T, {k: np.asarray(v) for k, v in ins.items()})
    nc = build_nc(T)
    in_maps = []
    for b in range(B):
        m = dict(base)
        m["x"] = np.ascontiguousarray(x[b])
        in_maps.append(m)
    res = run_bass_kernel_spmd(nc, in_maps, core_ids=list(range(B)))
    return np.stack([np.asarray(r["out"]) for r in res.results], 0)


# version 11
# speedup vs baseline: 1.0283x; 1.0283x over previous
import numpy as np
import ml_dtypes
from contextlib import ExitStack

import concourse.bass as bass
import concourse.mybir as mybir
from concourse.bass_utils import run_bass_kernel_spmd
from concourse.alu_op_type import AluOpType

F32 = mybir.dt.float32
BF16 = mybir.dt.bfloat16

def _act_set_id(name):
    from concourse.hw_specs import get_activation_tables
    import concourse.bass as _b
    _nc = _b.Bass("TRN2", target_bir_lowering=False)
    return list(get_activation_tables(_nc.m.arch)).index(name)

LNEXP_SET = None
SILU_SET = None
AF = mybir.ActivationFunctionType
AX = mybir.AxisListType.X

H = 8
DH = 128
DM = 1024
DC = 8
HF = 1408
HC = 11
NE = 9
EPS_LN = 1e-5
EPS_RMS = 1e-6
NEG_BIG = -1.0e30

W13_SL = [(h, min(h + 2, HC)) for h in range(0, HC, 2)]
W2_GR = [(0, 3), (3, 6), (6, 9)]


class Ctr:

    def __init__(self, ctx, nc, name):
        self.h = ctx.enter_context(nc.semaphore(name))
        self.n = 0

    def inc(self, instr, k=1):
        instr.then_inc(self.h, k)
        self.n += k
        return instr

    def wait(self, eng, n=None):
        eng.wait_ge(self.h, self.n if n is None else n)


def build_nc(T, debug=False):
    from concourse.hw_specs import get_activation_tables
    NT = T // 128
    NQ = T // 512
    nc = bass.Bass("TRN2", target_bir_lowering=False, debug=debug)
    tab_names = list(get_activation_tables(nc.m.arch))
    LNEXP_SET = tab_names.index("natural_log_exp_and_others")
    SILU_SET = tab_names.index("silu_and_others")

    x_in = nc.dram_tensor("x", [T, DM], F32, kind="ExternalInput").ap()
    wq_in = nc.dram_tensor("wq", [DM, DM], F32, kind="ExternalInput").ap()
    wk_in = nc.dram_tensor("wk", [DM, DM], F32, kind="ExternalInput").ap()
    wv_in = nc.dram_tensor("wv", [DM, DM], F32, kind="ExternalInput").ap()
    wo_in = nc.dram_tensor("wo", [H, DH, DM], F32, kind="ExternalInput").ap()
    w13_in = nc.dram_tensor("w13", [NE, HC, 128, 2, DC, 128], F32, kind="ExternalInput").ap()
    w2_in = nc.dram_tensor("w2", [DC, 128, NE, HC, 128], F32, kind="ExternalInput").ap()
    b13_in = nc.dram_tensor("b13", [128, NE, HC, 2], F32, kind="ExternalInput").ap()
    b2_in = nc.dram_tensor("b2", [8, DM], F32, kind="ExternalInput").ap()
    sb2_in = nc.dram_tensor("sb2c", [128, DC], F32, kind="ExternalInput").ap()
    gwt_in = nc.dram_tensor("gwt", [128, DC, 8], F32, kind="ExternalInput").ap()
    gbb_in = nc.dram_tensor("gbb", [128, 8], F32, kind="ExternalInput").ap()
    cos_in = nc.dram_tensor("cosT", [NT, 128, 64], BF16, kind="ExternalInput").ap()
    sin_in = nc.dram_tensor("sinT", [NT, 128, 64], BF16, kind="ExternalInput").ap()
    identb_in = nc.dram_tensor("identb", [128, 128], BF16, kind="ExternalInput").ap()
    identf_in = nc.dram_tensor("identf", [128, 128], F32, kind="ExternalInput").ap()
    ones1b_in = nc.dram_tensor("ones1b", [1, 128], BF16, kind="ExternalInput").ap()
    onescb_in = nc.dram_tensor("onescb", [128, 1], BF16, kind="ExternalInput").ap()
    ones1f_in = nc.dram_tensor("ones1f", [1, 128], F32, kind="ExternalInput").ap()
    masks_in = nc.dram_tensor("masksb", [128, 4, 512], BF16, kind="ExternalInput").ap()
    sel8_in = nc.dram_tensor("sel8", [8, 8, 128], BF16, kind="ExternalInput").ap()
    ccol_in = nc.dram_tensor("ccols", [128, 4], F32, kind="ExternalInput").ap()

    out_d = nc.dram_tensor("out", [T, DM], F32, kind="ExternalOutput").ap()

    w13s = nc.dram_tensor("w13s", [NE, HC, 128, 2, DC, 128], BF16).ap()
    w2s = nc.dram_tensor("w2s", [DC, 128, NE, HC, 128], BF16).ap()
    qs = nc.dram_tensor("qscr", [NT, 128, H, 128], BF16).ap()
    attnscr = nc.dram_tensor("attnscr", [NQ, DC, 128, 512], F32).ap()
    x1s = nc.dram_tensor("x1scr", [T, DM], F32).ap()
    xn2s = nc.dram_tensor("xn2scr", [NQ, 128, DC, 512], BF16).ap()
    rws = nc.dram_tensor("rwscr", [8, T], BF16).ap()

    with ExitStack() as top:
        def sbuf(name, shape, dt, stack):
            return stack.enter_context(nc.sbuf_tensor(name, shape, dt))

        def psum(name, shape, dt, stack):
            return stack.enter_context(nc.psum_tensor(name, shape, dt))

        s_identb = sbuf("s_identb", [128, 128], BF16, top)
        s_identf = sbuf("s_identf", [128, 128], F32, top)
        s_ones1b = sbuf("s_ones1b", [1, 128], BF16, top)
        s_onescb = sbuf("s_onescb", [128, 1], BF16, top)
        s_ccol = sbuf("s_ccol", [128, 4], F32, top)
        s_b13 = sbuf("s_b13", [128, NE, HC, 2], F32, top)
        s_sb2 = sbuf("s_sb2", [128, DC], F32, top)
        s_gwt = sbuf("s_gwt", [128, DC, 8], BF16, top)
        s_gbb = sbuf("s_gbb", [128, 8], F32, top)

        c_epsln = s_ccol[:, 0:1]
        c_epsrms = s_ccol[:, 1:2]
        c_neg1 = s_ccol[:, 2:3]
        c_negbig = s_ccol[:, 3:4]

        cast_sems = {}
        for cname in (["wqkv", "wo", "gate", "b2"] + [f"w13_{e}" for e in range(NE)]
                      + [f"w2_{mc}" for mc in range(DC)]):
            cast_sems[cname] = Ctr(top, nc, "cast_" + cname)

        def cast_wait(eng, cname):
            s = cast_sems[cname]
            eng.wait_ge(s.h, s.n)

        sems = {}
        for name in ["c0",
                     "v1", "s1", "t1",
                     "v2", "s2", "t2",
                     "v25", "s25", "t25",
                     "v3", "s3", "t3",
                     "yLd0", "yLd1", "yQs",
                     "yQq", "yAt0", "yAt1",
                     "yL25a", "yL25b", "yX1", "yXn2",
                     "yW13a", "yW13b", "yW13c", "yW2a", "yW2b", "yW2c",
                     "yXn2f", "yX1t", "yOut", "yRw", "yRwF"]:
            sems[name] = Ctr(top, nc, "sem_" + name)
        c0 = sems["c0"]

        with ExitStack() as p012:
            k_sb = sbuf("k_sb", [128, H, NT, 128], BF16, p012)
            v_sb = sbuf("v_sb", [128, H, NT, 128], BF16, p012)

            v1, s1, t1 = sems["v1"], sems["s1"], sems["t1"]
            yLd = [sems["yLd0"], sems["yLd1"]]
            yQs = sems["yQs"]
            with ExitStack() as p1:
                wq_sb = sbuf("wq_sb", [128, DC, DM], BF16, p1)
                wk_sb = sbuf("wk_sb", [128, DC, DM], BF16, p1)
                wv_sb = sbuf("wv_sb", [128, DC, DM], BF16, p1)
                xb = [sbuf(f"xb{i}", [128, DM], F32, p1) for i in range(2)]
                cb = [sbuf(f"cb{i}", [128, 64], BF16, p1) for i in range(2)]
                sb_ = [sbuf(f"sb{i}", [128, 64], BF16, p1) for i in range(2)]
                mv = sbuf("mv", [128, 14], F32, p1)
                xn_tm = sbuf("xn_tm", [128, DM], BF16, p1)
                xn_fm = sbuf("xn_fm", [128, DM], BF16, p1)
                sqb = sbuf("sqb", [128, DM], F32, p1)
                qkb_q = sbuf("qkb_q", [128, DM], BF16, p1)
                qkb_k = sbuf("qkb_k", [128, DM], BF16, p1)
                rt_q = sbuf("rt_q", [128, DM], BF16, p1)
                rt_k = sbuf("rt_k", [128, DM], BF16, p1)
                q_rope = sbuf("q_rope", [128, DM], BF16, p1)
                k_rope = sbuf("k_rope", [128, DM], BF16, p1)
                qfm_st = rt_q
                msq = sbuf("msq", [128, 8], F32, p1)
                rsq = sbuf("rsq", [128, 8], F32, p1)
                msq2 = sbuf("msq2", [128, 8], F32, p1)
                rsq2 = sbuf("rsq2", [128, 8], F32, p1)
                psQ = psum("psQ", [128, DM], F32, p1)
                psK = psum("psK", [128, DM], F32, p1)
                psV = psum("psV", [128, DM], F32, p1)
                psXP = psum("psXP", [128, DM], BF16, p1)
                psT2 = psum("psT2", [128, DM], BF16, p1)

                ck = {}

                with nc.Block("p1") as blk:
                    @blk.scalar
                    def _(s):
                        s.add_instruction(mybir.InstLoadActFuncSet(
                            name=nc.get_next_instruction_name(),
                            act_func_set_id=LNEXP_SET, ins=[], outs=[]))
                        c0.wait(s, 128)

                    @blk.sync
                    def _(sy):
                        for dst, srcap in [
                            (s_identb[:], identb_in[:]), (s_identf[:], identf_in[:]),
                            (s_ones1b[:], ones1b_in[:]), (s_onescb[:], onescb_in[:]),
                            (s_ccol[:], ccol_in[:]), (s_b13[:], b13_in[:]),
                            (s_sb2[:], sb2_in[:]), (s_gbb[:], gbb_in[:]),
                        ]:
                            c0.inc(sy.dma_start(dst, srcap), 16)

                    @blk.gpsimd
                    def _(g):
                        cq = cast_sems["wqkv"]
                        cq.inc(g.dma_start(wq_sb[:], wq_in.rearrange("(c p) n -> p c n", p=128)), 16)
                        cq.inc(g.dma_start(wk_sb[:], wk_in.rearrange("(c p) n -> p c n", p=128)), 16)
                        cq.inc(g.dma_start(wv_sb[:], wv_in.rearrange("(c p) n -> p c n", p=128)), 16)
                        cast_sems["gate"].inc(g.dma_start(s_gwt[:], gwt_in[:]), 16)
                        for e in range(NE):
                            cast_sems[f"w13_{e}"].inc(g.dma_start(w13s[e], w13_in[e]), 16)
                        for mc in range(DC):
                            cast_sems[f"w2_{mc}"].inc(g.dma_start(w2s[mc], w2_in[mc]), 16)

                    @blk.vector
                    def _(v):
                        c0.wait(v, 128)

                    @blk.tensor
                    def _(tt):
                        c0.wait(tt, 128)

                    def emit_qk_T(t):
                        for which, ro in [("q", q_rope), ("k", k_rope)]:
                            @blk.tensor
                            def _(tt, t=t, which=which, ro=ro):
                                v1.wait(tt, ck[("rope", t, which)])
                                if not (t == 0 and which == "q"):
                                    v1.wait(tt, ck[("psT2",)])
                                for h in range(H):
                                    mm = tt.transpose(psT2[:, h * 128:(h + 1) * 128],
                                                      ro[:, h * 128:(h + 1) * 128], s_identb[:])
                                t1.inc(mm)
                                ck[("qT" if which == "q" else "kT", t)] = t1.n

                            @blk.vector
                            def _(v, t=t, which=which):
                                t1.wait(v)
                                if which == "q":
                                    if t >= 1:
                                        yQs.wait(v, 16 * t)
                                    v1.inc(v.tensor_copy(qfm_st[:], psT2[:]))
                                else:
                                    v1.inc(v.tensor_copy(
                                        k_sb[:, :, t, :],
                                        psT2[:].rearrange("p (h d) -> p h d", h=H)))
                                ck[("psT2",)] = v1.n
                                if which == "q":
                                    ck[("qfm", t)] = v1.n

                        @blk.sync
                        def _(sy, t=t):
                            v1.wait(sy, ck[("qfm", t)])
                            yQs.inc(sy.dma_start(qs[t].rearrange("p h d -> p (h d)"), qfm_st[:]), 16)

                    def emit_loads(t):
                        p = t % 2

                        @blk.sync
                        def _(sy, t=t, p=p):
                            if t >= 2:
                                v1.wait(sy, ck[("ropek", t - 2)])
                                s1.wait(sy, ck[("apply", t - 2)])
                            yLd[p].inc(sy.dma_start(xb[p][:], x_in[t * 128:(t + 1) * 128, :]), 16)
                            yLd[p].inc(sy.dma_start(cb[p][:], cos_in[t]), 16)
                            yLd[p].inc(sy.dma_start(sb_[p][:], sin_in[t]), 16)
                            ck[("xld", t)] = yLd[p].n

                    def emit_ln(t):
                        p = t % 2

                        @blk.vector
                        def _(v, t=t, p=p):
                            yLd[p].wait(v, ck[("xld", t)])
                            v.bn_stats(mv[:, 0:6], xb[p][:, 0:512])
                            v.bn_stats(mv[:, 6:12], xb[p][:, 512:1024])
                            v.drain()
                            v.bn_aggr(mv[:, 12:14], mv[:, 0:12])
                            v.drain()
                            v.tensor_scalar(mv[:, 13:14], mv[:, 13:14], c_epsln,
                                            None, AluOpType.add)
                            v.drain()
                            v1.inc(v.tensor_scalar(mv[:, 11:12], mv[:, 12:13], c_neg1,
                                                   None, AluOpType.mult))
                            ck[("stats", t)] = v1.n

                        @blk.scalar
                        def _(s, t=t, p=p):
                            v1.wait(s, ck[("stats", t)])
                            s.activation(mv[:, 13:14], mv[:, 13:14], AF.Ln)
                            s.drain()
                            s.activation(mv[:, 13:14], mv[:, 13:14], AF.Exp, scale=-0.5)
                            s.drain()
                            s.activation(mv[:, 12:13], mv[:, 11:12], AF.Identity,
                                         scale=mv[:, 13:14])
                            s.drain()
                            if t >= 1:
                                t1.wait(s, ck[("xnT", t - 1)])
                            s1.inc(s.activation(xn_tm[:], xb[p][:], AF.Identity,
                                                bias=mv[:, 12:13], scale=mv[:, 13:14]))
                            ck[("apply", t)] = s1.n

                    emit_loads(0)
                    emit_ln(0)

                    def emit_xnT(t):
                        @blk.tensor
                        def _(tt, t=t):
                            s1.wait(tt, ck[("apply", t)])
                            if t >= 1:
                                v1.wait(tt, ck[("xnfm", t - 1)])
                            for i in range(8):
                                mmt = tt.transpose(psXP[:, i * 128:(i + 1) * 128],
                                                   xn_tm[:, i * 128:(i + 1) * 128], s_identb[:])
                            t1.inc(mmt)
                            ck[("xnT", t)] = t1.n

                        @blk.vector
                        def _(v, t=t):
                            t1.wait(v, ck[("xnT", t)])
                            if t >= 1:
                                t1.wait(v, ck[("qkv", t - 1)])
                            v1.inc(v.tensor_copy(xn_fm[:], psXP[:]))
                            ck[("xnfm", t)] = v1.n

                    emit_xnT(0)

                    for t in range(NT):
                        p = t % 2

                        if t + 1 < NT:
                            emit_loads(t + 1)

                        @blk.tensor
                        def _(tt, t=t):
                            if t == 0:
                                cast_wait(tt, "wqkv")
                            v1.wait(tt, ck[("xnfm", t)])
                            if t >= 1:
                                s1.wait(tt, ck[("kscale", t - 1)])
                                v1.wait(tt, ck[("vev", t - 1)])
                            for ps, w in [(psQ, wq_sb), (psK, wk_sb), (psV, wv_sb)]:
                                for hlf in (0, 512):
                                    for dc in range(DC):
                                        mm = tt.matmul(ps[:, hlf:hlf + 512],
                                                       xn_fm[:, dc * 128:(dc + 1) * 128],
                                                       w[:, dc, hlf:hlf + 512],
                                                       start=(dc == 0), stop=(dc == DC - 1))
                            t1.inc(mm)
                            ck[("qkv", t)] = t1.n

                        if t >= 1:
                            emit_qk_T(t - 1)

                        if t + 1 < NT:
                            emit_ln(t + 1)
                            emit_xnT(t + 1)

                        @blk.vector
                        def _(v, t=t):
                            t1.wait(v, ck[("qkv", t)])
                            v1.inc(v.tensor_copy(
                                v_sb[:, :, t, :],
                                psV[:].rearrange("p (h d) -> p h d", h=H)))
                            ck[("vev", t)] = v1.n

                        @blk.scalar
                        def _(s, t=t):
                            t1.wait(s, ck[("qkv", t)])
                            if t >= 1:
                                v1.wait(s, ck[("kred", t - 1)])
                            s1.inc(s.activation(sqb[:], psQ[:], AF.Square))
                            ck[("qsq", t)] = s1.n

                        @blk.vector
                        def _(v, t=t):
                            s1.wait(v, ck[("qsq", t)])
                            v1.inc(v.reduce_sum(
                                msq[:], sqb[:].rearrange("p (h d) -> p h d", h=H), axis=AX))
                            ck[("qred", t)] = v1.n

                        @blk.scalar
                        def _(s, t=t):
                            v1.wait(s, ck[("qred", t)])
                            s1.inc(s.activation(sqb[:], psK[:], AF.Square))
                            ck[("ksq", t)] = s1.n

                        @blk.scalar
                        def _(s, t=t):
                            s.activation(rsq[:], msq[:], AF.Ln, bias=c_epsrms, scale=1.0 / DH)
                            s.drain()
                            s.activation(rsq[:], rsq[:], AF.Exp, scale=-0.5)
                            s.drain()
                            for h in range(H):
                                mm = s.activation(qkb_q[:, h * 128:(h + 1) * 128],
                                                  psQ[:, h * 128:(h + 1) * 128],
                                                  AF.Identity, scale=rsq[:, h:h + 1])
                            s1.inc(mm)
                            ck[("qscale", t)] = s1.n

                        @blk.vector
                        def _(v, t=t):
                            s1.wait(v, ck[("ksq", t)])
                            v1.inc(v.reduce_sum(
                                msq2[:], sqb[:].rearrange("p (h d) -> p h d", h=H), axis=AX))
                            ck[("kred", t)] = v1.n

                        def emit_rope(t, which, ro, qkb, scale_ck):
                            p = t % 2

                            @blk.vector
                            def _(v, t=t, which=which, ro=ro, qkb=qkb):
                                s1.wait(v, ck[(scale_ck, t)])
                                if t >= 1:
                                    t1.wait(v, ck[("qT" if which == "q" else "kT", t - 1)])
                                if which == "q" and t >= 1:
                                    yQs.wait(v, 16 * t)
                                rt = rt_q if which == "q" else rt_k
                                r3 = lambda ap: ap.rearrange("p (h d) -> p h d", h=H)
                                ev, od = r3(qkb[:])[:, :, 0:64], r3(qkb[:])[:, :, 64:128]
                                oe, oo = r3(ro[:])[:, :, 0:64], r3(ro[:])[:, :, 64:128]
                                te, to = r3(rt[:])[:, :, 0:64], r3(rt[:])[:, :, 64:128]
                                cosb = bass.AP(cb[p].ap().tensor, cb[p].ap().offset,
                                               [cb[p].ap().ap[0], [0, H], [1, 64]])
                                sinb = bass.AP(sb_[p].ap().tensor, sb_[p].ap().offset,
                                               [sb_[p].ap().ap[0], [0, H], [1, 64]])
                                v.tensor_tensor(te, ev, cosb, AluOpType.mult)
                                v.tensor_tensor(to, od, sinb, AluOpType.mult)
                                v.drain()
                                v.tensor_tensor(oe, te, to, AluOpType.subtract)
                                v.drain()
                                v.tensor_tensor(te, ev, sinb, AluOpType.mult)
                                v.tensor_tensor(to, od, cosb, AluOpType.mult)
                                v.drain()
                                v1.inc(v.tensor_tensor(oo, te, to, AluOpType.add))
                                ck[("rope", t, which)] = v1.n
                                if which == "k":
                                    ck[("ropek", t)] = v1.n

                        emit_rope(t, "q", q_rope, qkb_q, "qscale")

                        @blk.scalar
                        def _(s, t=t):
                            v1.wait(s, ck[("kred", t)])
                            s.activation(rsq2[:], msq2[:], AF.Ln, bias=c_epsrms, scale=1.0 / DH)
                            s.drain()
                            s.activation(rsq2[:], rsq2[:], AF.Exp, scale=-0.5)
                            s.drain()
                            for h in range(H):
                                mm = s.activation(qkb_k[:, h * 128:(h + 1) * 128],
                                                  psK[:, h * 128:(h + 1) * 128],
                                                  AF.Identity, scale=rsq2[:, h:h + 1])
                            s1.inc(mm)
                            ck[("kscale", t)] = s1.n

                        emit_rope(t, "k", k_rope, qkb_k, "kscale")

                    emit_qk_T(NT - 1)


            v2, s2, t2 = sems["v2"], sems["s2"], sems["t2"]
            yQq = sems["yQq"]
            yAt = [sems["yAt0"], sems["yAt1"]]
            with ExitStack() as p2:
                s_masks = sbuf("s_masks", [128, 4, 512], BF16, p2)
                s_ones1f = sbuf("s_ones1f", [1, 128], F32, p2)
                wo_sb = sbuf("wo_sb", [128, H, DC, 128], BF16, p2)
                qq = sbuf("qq", [128, 4, H, 128], BF16, p2)
                pt_sl = [sbuf(f"pt{i}", [128, 1024], BF16, p2) for i in range(2)]
                ptm = sbuf("ptm", [128, 4, 512], BF16, p2)
                sden = [sbuf(f"sden{i}", [1, 512], F32, p2) for i in range(2)]
                srden = [sbuf(f"srden{i}", [1, 512], F32, p2) for i in range(2)]
                sdb = [sbuf(f"sdb{i}", [128, 512], F32, p2) for i in range(2)]
                o_all = sbuf("o_all", [128, H, 512], BF16, p2)
                sattn = [sbuf(f"sattn{i}", [128, 512], F32, p2) for i in range(2)]
                psS = [psum(f"psS{i}", [128, 1024], F32, p2) for i in range(2)]
                psO = [psum(f"psO{i}", [128, 512], F32, p2) for i in range(2)]
                psDD = [psum(f"psDD{i}", [128, 512], F32, p2) for i in range(2)]
                psA = psS[0][:, 0:512]

                ck2 = {"exp_last": {}, "pv_last": {}, "onorm_p": {}, "wodone": 0}

                with nc.Block("p2") as blk:
                    @blk.gpsimd
                    def _(g):
                        cast_sems["wo"].inc(g.dma_start(
                            wo_sb[:],
                            bass.AP(wo_in.tensor, 0,
                                    [[1024, 128], [131072, H], [128, DC], [1, 128]])), 16)

                    @blk.scalar
                    def _(s):
                        s.add_instruction(mybir.InstLoadActFuncSet(
                            name=nc.get_next_instruction_name(),
                            act_func_set_id=LNEXP_SET, ins=[], outs=[]))

                    @blk.sync
                    def _(sy):
                        yQs.wait(sy)
                        yLd[0].wait(sy)
                        yLd[1].wait(sy)
                        c0.inc(sy.dma_start(s_masks[:], masks_in[:]), 16)
                        c0.inc(sy.dma_start(s_ones1f[:], ones1f_in[:]), 16)

                    for qg in range(NQ):
                        JC = 4 * (qg + 1)
                        nbatch = (JC + 1) // 2

                        @blk.sync
                        def _(sy, qg=qg):
                            if qg >= 1:
                                t2.wait(sy, ck2[("qq_free", qg - 1)])
                            for i in range(4):
                                yQq.inc(sy.dma_start(qq[:, i, :, :], qs[4 * qg + i]), 16)

                        for h in range(H):
                            hp = h % 2
                            exp_ck, pv_ck = {}, {}
                            batches = [list(range(2 * b, min(2 * b + 2, JC)))
                                       for b in range(nbatch)]

                            def emit_st(b, qg=qg, h=h):
                                jcs = batches[b]

                                @blk.tensor
                                def _(tt):
                                    if b == 0 and h == 0:
                                        yQq.wait(tt)
                                    if (b - 2) in exp_ck:
                                        s2.wait(tt, exp_ck[b - 2])
                                    elif b < 2 and (b % 2) in ck2["exp_last"]:
                                        s2.wait(tt, ck2["exp_last"][b % 2])
                                    for i, jc in enumerate(jcs):
                                        mm = tt.matmul(psS[b % 2][:, i * 512:(i + 1) * 512],
                                                       k_sb[:, h, jc, :], qq[:, :, h, :],
                                                       start=True, stop=True)
                                    t2.inc(mm)
                                    ck2[("st", b)] = t2.n

                            def emit_exp(b, qg=qg, h=h):
                                jcs = batches[b]

                                @blk.scalar
                                def _(s):
                                    t2.wait(s, ck2[("st", b)])
                                    if (b - 2) in pv_ck:
                                        t2.wait(s, pv_ck[b - 2])
                                    elif b < 2 and (b % 2) in ck2["pv_last"]:
                                        t2.wait(s, ck2["pv_last"][b % 2])
                                    s2.inc(s.activation(pt_sl[b % 2][:, 0:len(jcs) * 512],
                                                        psS[b % 2][:, 0:len(jcs) * 512],
                                                        AF.Exp, scale=1.0 / DH))
                                    exp_ck[b] = s2.n
                                    ck2["exp_last"][b % 2] = s2.n

                                diag = [i for i, jc in enumerate(jcs) if jc >= 4 * qg]
                                if diag:
                                    @blk.vector
                                    def _(v):
                                        if qg == 0 and h == 0:
                                            c0.wait(v)
                                        s2.wait(v, exp_ck[b])
                                        for i in diag:
                                            kk = jcs[i] - 4 * qg
                                            mm = v.tensor_tensor(
                                                ptm[:, kk, :],
                                                pt_sl[b % 2][:, i * 512:(i + 1) * 512],
                                                s_masks[:, kk, :], AluOpType.mult)
                                        v2.inc(mm)
                                        ck2[("mask", b)] = v2.n

                            def emit_pv(b, qg=qg, h=h, hp=hp):
                                jcs = batches[b]

                                @blk.tensor
                                def _(tt):
                                    s2.wait(tt, exp_ck[b])
                                    if any(jc >= 4 * qg for jc in jcs):
                                        v2.wait(tt, ck2[("mask", b)])
                                    if b == 0 and (hp in ck2["onorm_p"]):
                                        v2.wait(tt, ck2["onorm_p"][hp])
                                    for i, jc in enumerate(jcs):
                                        src = (ptm[:, jc - 4 * qg, :] if jc >= 4 * qg
                                               else pt_sl[b % 2][:, i * 512:(i + 1) * 512])
                                        tt.matmul(psO[hp][:], v_sb[:, h, jc, :], src,
                                                  start=(jc == 0), stop=(jc == JC - 1))
                                        mm = tt.matmul(psDD[hp][0:1, :], s_onescb[:], src,
                                                       start=(jc == 0), stop=(jc == JC - 1))
                                    t2.inc(mm)
                                    pv_ck[b] = t2.n
                                    ck2["pv_last"][b % 2] = t2.n

                            emit_st(0)
                            emit_exp(0)
                            for b in range(1, nbatch):
                                emit_st(b)
                                emit_pv(b - 1)
                                emit_exp(b)
                            emit_pv(nbatch - 1)

                            @blk.vector
                            def _(v, qg=qg, h=h, hp=hp, pv_ck=pv_ck, nbatch=nbatch):
                                t2.wait(v, pv_ck[nbatch - 1])
                                if qg >= 1 and h == 0:
                                    t2.wait(v, ck2["wodone"])
                                v.tensor_copy(sden[hp][:], psDD[hp][0:1, :])
                                v.drain()
                                v2.inc(v.reciprocal(srden[hp][:], sden[hp][:]))

                            @blk.tensor
                            def _(tt, qg=qg, h=h, hp=hp):
                                if qg == 0 and h == 0:
                                    c0.wait(tt)
                                v2.wait(tt)
                                t2.inc(tt.matmul(psDD[hp][:], s_ones1f[:], srden[hp][:],
                                                 start=True, stop=True))

                            @blk.vector
                            def _(v, qg=qg, h=h, hp=hp):
                                t2.wait(v)
                                v.tensor_copy(sdb[hp][:], psDD[hp][:])
                                v.drain()
                                v2.inc(v.tensor_tensor(o_all[:, h, :], psO[hp][:], sdb[hp][:],
                                                       AluOpType.mult))
                                ck2["onorm_p"][hp] = v2.n
                                if h == H - 1:
                                    ck2[("heads_done", qg)] = v2.n

                        ck2[("qq_free", qg)] = t2.n

                        for mc in range(DC):
                            @blk.tensor
                            def _(tt, qg=qg, mc=mc):
                                if qg == 0 and mc == 0:
                                    cast_wait(tt, "wo")
                                v2.wait(tt, ck2[("heads_done", qg)])
                                if ("attn_ev", qg, mc - 1) in ck2:
                                    v2.wait(tt, ck2[("attn_ev", qg, mc - 1)])
                                elif qg >= 1 and mc == 0:
                                    v2.wait(tt, ck2[("attn_ev", qg - 1, DC - 1)])
                                for h in range(H):
                                    mm = tt.matmul(psA, wo_sb[:, h, mc, :], o_all[:, h, :],
                                                   start=(h == 0), stop=(h == H - 1))
                                t2.inc(mm)
                                if mc == DC - 1:
                                    ck2["wodone"] = t2.n

                            @blk.vector
                            def _(v, qg=qg, mc=mc):
                                t2.wait(v)
                                k_st = qg * DC + mc
                                if k_st >= 2:
                                    yAt[mc % 2].wait(v, 16 * (k_st // 2))
                                v2.inc(v.tensor_copy(sattn[mc % 2][:], psA))
                                ck2[("attn_ev", qg, mc)] = v2.n

                            @blk.sync
                            def _(sy, qg=qg, mc=mc):
                                v2.wait(sy, ck2[("attn_ev", qg, mc)])
                                yAt[mc % 2].inc(sy.dma_start(attnscr[qg, mc], sattn[mc % 2][:]), 16)

            v25, s25, t25 = sems["v25"], sems["s25"], sems["t25"]
            yL25 = [sems["yL25a"], sems["yL25b"]]
            yX1 = sems["yX1"]
            yXn2 = sems["yXn2"]
            yRw = sems["yRw"]
            with ExitStack() as p25:
                xb2 = [sbuf(f"xb2_{i}", [128, DM], F32, p25) for i in range(2)]
                afm = [sbuf(f"afm{i}", [128, DC, 128], F32, p25) for i in range(2)]
                x1tm = sbuf("x1tm", [128, DM], F32, p25)
                xn2tm = sbuf("xn2tm", [128, DM], BF16, p25)
                xn2st = [sbuf(f"xn2st{i}", [128, DC, 512], BF16, p25) for i in range(2)]
                mv2 = sbuf("mv2", [128, 14], F32, p25)
                sxp = sbuf("sxp", [128, 8], F32, p25)
                sume = sbuf("sume", [128, 1], F32, p25)
                rsum = sbuf("rsum", [128, 1], F32, p25)
                rsc = sbuf("rsc", [128, 8], F32, p25)
                bsd = sbuf("bsd", [128, 8], F32, p25)
                m1c = sbuf("m1c", [128, 1], F32, p25)
                mask1 = sbuf("mask1", [128, 8], F32, p25)
                b2d = sbuf("b2d", [128, 8], F32, p25)
                m2c = sbuf("m2c", [128, 1], F32, p25)
                mask2 = sbuf("mask2", [128, 8], F32, p25)
                rwbf = sbuf("rwbf", [128, 8], BF16, p25)
                rw_st = sbuf("rw_st", [8, 128], BF16, p25)
                psT25 = psum("psT25", [128, DM], F32, p25)
                psX2 = psum("psX2", [128, DM], BF16, p25)
                psG = psum("psG", [128, 512], F32, p25)
                psRw = psum("psRw", [8, 512], BF16, p25)

                ck25 = {}

                with nc.Block("p25") as blk:
                    @blk.sync
                    def _(sy):
                        yAt[0].wait(sy)
                        yAt[1].wait(sy)
                        yQq.wait(sy)

                    for t in range(NT):
                        p = t % 2
                        qg, off = t // 4, (t % 4) * 128

                        @blk.sync
                        def _(sy, t=t, p=p, qg=qg, off=off):
                            if t >= 2:
                                v25.wait(sy, ck25[("vfin", t - 2)])
                            yL25[p].inc(sy.dma_start(xb2[p][:], x_in[t * 128:(t + 1) * 128, :]), 16)
                            yL25[p].inc(sy.dma_start(
                                afm[p][:],
                                bass.AP(attnscr.tensor, qg * DC * 128 * 512 + off,
                                        [[512, 128], [128 * 512, DC], [1, 128]])), 16)
                            ck25[("ld", t)] = yL25[p].n

                        @blk.tensor
                        def _(tt, t=t, p=p):
                            yL25[p].wait(tt, ck25[("ld", t)])
                            if t >= 1:
                                v25.wait(tt, ck25[("x1add", t - 1)])
                            for mc in range(DC):
                                mm = tt.transpose(psT25[:, mc * 128:(mc + 1) * 128],
                                                  afm[p][:, mc, :], s_identf[:])
                            t25.inc(mm)

                        @blk.vector
                        def _(v, t=t, p=p):
                            t25.wait(v)
                            if t >= 1:
                                yX1.wait(v, 16 * t)
                            v25.inc(v.tensor_tensor(x1tm[:], psT25[:], xb2[p][:], AluOpType.add))
                            ck25[("x1add", t)] = v25.n

                        @blk.sync
                        def _(sy, t=t):
                            v25.wait(sy, ck25[("x1add", t)])
                            yX1.inc(sy.dma_start(x1s[t * 128:(t + 1) * 128, :], x1tm[:]), 16)

                        @blk.vector
                        def _(v, t=t):
                            v.drain()
                            v.bn_stats(mv2[:, 0:6], x1tm[:, 0:512])
                            v.bn_stats(mv2[:, 6:12], x1tm[:, 512:1024])
                            v.drain()
                            v.bn_aggr(mv2[:, 12:14], mv2[:, 0:12])
                            v.drain()
                            v.tensor_scalar(mv2[:, 13:14], mv2[:, 13:14],
                                            c_epsln, None, AluOpType.add)
                            v.drain()
                            v25.inc(v.tensor_scalar(mv2[:, 11:12], mv2[:, 12:13], c_neg1,
                                                    None, AluOpType.mult))

                        @blk.scalar
                        def _(s, t=t):
                            v25.wait(s)
                            s.activation(mv2[:, 13:14], mv2[:, 13:14], AF.Ln)
                            s.drain()
                            s.activation(mv2[:, 13:14], mv2[:, 13:14], AF.Exp, scale=-0.5)
                            s.drain()
                            s.activation(mv2[:, 12:13], mv2[:, 11:12], AF.Identity,
                                         scale=mv2[:, 13:14])
                            s.drain()
                            if t >= 1:
                                t25.wait(s, ck25[("x2t", t - 1)])
                            s25.inc(s.activation(xn2tm[:], x1tm[:], AF.Identity,
                                                 bias=mv2[:, 12:13], scale=mv2[:, 13:14]))
                            ck25[("apply2", t)] = s25.n

                        @blk.tensor
                        def _(tt, t=t):
                            s25.wait(tt, ck25[("apply2", t)])
                            if t >= 1:
                                v25.wait(tt, ck25[("x2ev", t - 1)])
                            for i in range(DC):
                                mm = tt.transpose(psX2[:, i * 128:(i + 1) * 128],
                                                  xn2tm[:, i * 128:(i + 1) * 128], s_identb[:])
                            t25.inc(mm)
                            ck25[("x2t", t)] = t25.n

                        @blk.vector
                        def _(v, t=t, qg=qg, off=off):
                            t25.wait(v, ck25[("x2t", t)])
                            if t % 4 == 0 and qg >= 2:
                                yXn2.wait(v, 16 * qg)
                                t25.wait(v, ck25[("gmm", 4 * (qg - 2) + 3)])
                            v25.inc(v.tensor_copy(
                                xn2st[qg % 2][:, :, off:off + 128],
                                psX2[:].rearrange("p (c d) -> p c d", c=DC)))
                            ck25[("x2ev", t)] = v25.n

                        @blk.sync
                        def _(sy, t=t, qg=qg):
                            if t % 4 == 3:
                                v25.wait(sy, ck25[("x2ev", t)])
                                yXn2.inc(sy.dma_start(xn2s[qg], xn2st[qg % 2][:]), 16)

                        @blk.tensor
                        def _(tt, t=t, qg=qg, off=off):
                            if t == 0:
                                cast_wait(tt, "gate")
                            v25.wait(tt, ck25[("x2ev", t)])
                            if t >= 1:
                                s25.wait(tt, ck25[("gexp", t - 1)])
                            for dc in range(DC):
                                mm = tt.matmul(psG[:, 0:8],
                                               xn2st[qg % 2][:, dc, off:off + 128],
                                               s_gwt[:, dc, :],
                                               start=(dc == 0), stop=(dc == DC - 1))
                            t25.inc(mm)
                            ck25[("gmm", t)] = t25.n

                        @blk.vector
                        def _(v, t=t):
                            t25.wait(v, ck25[("gmm", t)])
                            v25.inc(v.reduce_max(m1c[:], psG[:, 0:8], axis=AX, negate=True))

                        @blk.scalar
                        def _(s, t=t):
                            v25.wait(s)
                            s25.inc(s.activation(sxp[:], psG[:, 0:8], AF.Exp,
                                                 bias=m1c[:], accum_out=sume[:]))
                            ck25[("gexp", t)] = s25.n

                        @blk.vector
                        def _(v, t=t):
                            s25.wait(v, ck25[("gexp", t)])
                            v.reciprocal(rsum[:], sume[:])
                            v.drain()
                            v.tensor_scalar(rsc[:], sxp[:], rsum[:], None, AluOpType.mult)
                            v.drain()
                            v.tensor_tensor(bsd[:], rsc[:], s_gbb[:], AluOpType.add)
                            v.drain()
                            v.reduce_max(m1c[:], bsd[:], axis=AX)
                            v.drain()
                            v.tensor_scalar(mask1[:], bsd[:], m1c[:], None, AluOpType.is_ge)
                            v.drain()
                            v.scalar_tensor_tensor(b2d[:], mask1[:], c_negbig, bsd[:],
                                                   AluOpType.mult, AluOpType.add)
                            v.drain()
                            v.reduce_max(m2c[:], b2d[:], axis=AX)
                            v.drain()
                            v.tensor_scalar(mask2[:], b2d[:], m2c[:], None, AluOpType.is_ge)
                            v.drain()
                            v.tensor_tensor(mask1[:], mask1[:], mask2[:], AluOpType.add)
                            v.drain()
                            v25.inc(v.tensor_tensor(rwbf[:], rsc[:], mask1[:], AluOpType.mult))

                        @blk.tensor
                        def _(tt, t=t):
                            v25.wait(tt)
                            if t >= 1:
                                v25.wait(tt, ck25[("rwev", t - 1)])
                            t25.inc(tt.transpose(psRw[:, 0:128], rwbf[:], s_identb[:]))

                        @blk.vector
                        def _(v, t=t):
                            t25.wait(v)
                            if t >= 1:
                                yRw.wait(v, 16 * t)
                            v25.inc(v.tensor_copy(rw_st[:], psRw[:, 0:128]))
                            ck25[("rwev", t)] = v25.n

                        @blk.sync
                        def _(sy, t=t):
                            v25.wait(sy, ck25[("rwev", t)])
                            yRw.inc(sy.dma_start(
                                bass.AP(rws.tensor, t * 128, [[T, 8], [1, 128]]),
                                rw_st[:]), 16)

                        ck25[("vfin", t)] = v25.n

        v3, s3, t3 = sems["v3"], sems["s3"], sems["t3"]
        yRw = sems["yRw"]
        yRwF = sems["yRwF"]
        yW13 = [sems["yW13a"], sems["yW13b"], sems["yW13c"]]
        yW2 = [sems["yW2a"], sems["yW2b"], sems["yW2c"]]
        yXn2f = sems["yXn2f"]
        yX1t = sems["yX1t"]
        yOut = sems["yOut"]
        with ExitStack() as p3:
            h_all = sbuf("h_all", [128, NE, HC, 512], BF16, p3)
            s_b2 = sbuf("s_b2", [8, DM], BF16, p3)
            s_sel8 = sbuf("s_sel8", [8, 8, 128], BF16, p3)
            w13b = [sbuf(f"w13b{i}", [128, 2 * 2048], BF16, p3) for i in range(3)]
            w2b = [sbuf(f"w2b{i}", [128, 3 * HC * 128], BF16, p3) for i in range(3)]
            xn2f = sbuf("xn2f", [128, DC, 512], BF16, p3)
            x1t = sbuf("x1t", [128, 4, DM], F32, p3)
            rwT_f = sbuf("rwT_f", [8, 512], BF16, p3)
            rwb = [sbuf(f"rwb{i}", [128, 512], BF16, p3) for i in range(2)]
            sA = [sbuf(f"sA{i}", [128, 512], F32, p3) for i in range(2)]
            sProd = sbuf("sProd", [128, 4, 512], F32, p3)
            htmp4 = sbuf("htmp4", [128, 2048], BF16, p3)
            yf = [sbuf(f"yf{i}", [128, 512], F32, p3) for i in range(2)]
            ost = [sbuf(f"ost{i}", [128, 4, 128], F32, p3) for i in range(1)]
            psA3 = [psum(f"psA3{i}", [128, 512], F32, p3) for i in range(2)]
            psB3 = [psum(f"psB3{i}", [128, 512], F32, p3) for i in range(2)]
            psY = [psum(f"psY{i}", [128, 512], F32, p3) for i in range(2)]
            psYT = psum("psYT", [128, 512], F32, p3)
            psR = psum("psR", [128, 512], F32, p3)

            NSL = len(W13_SL)
            n_sl = NQ * NE * NSL

            def sl_idx(f, e, si):
                return (f * NE + e) * NSL + si

            def sl_of(k):
                f, r = divmod(k, NE * NSL)
                e, si = divmod(r, NSL)
                return f, e, si

            def w13_ap(e, si):
                h0, h1 = W13_SL[si]
                base = (e * HC + h0) * (128 * 2048)
                return bass.AP(w13s.tensor, base,
                               [[2048, 128], [128 * 2048, h1 - h0], [1, 2048]])

            def w2_ap(mc, gi):
                e0, e1 = W2_GR[gi]
                base = mc * (128 * NE * HC * 128) + e0 * (HC * 128)
                return bass.AP(w2s.tensor, base,
                               [[NE * HC * 128, 128], [1, (e1 - e0) * HC * 128]])

            ck3 = {}

            with nc.Block("p3") as blk:
                @blk.gpsimd
                def _(g):
                    cast_sems["b2"].inc(g.dma_start(s_b2[:], b2_in[:]), 16)

                @blk.scalar
                def _(s):
                    s.add_instruction(mybir.InstLoadActFuncSet(
                        name=nc.get_next_instruction_name(),
                        act_func_set_id=SILU_SET, ins=[], outs=[]))

                @blk.sync
                def _(sy):
                    yX1.wait(sy)
                    yXn2.wait(sy)
                    yRw.wait(sy)
                    yL25[0].wait(sy)
                    yL25[1].wait(sy)
                    c0.inc(sy.dma_start(s_sel8[:], sel8_in[:]), 16)
                    for k in range(3):
                        f_, e_, si_ = sl_of(k)
                        cast_wait(sy, f"w13_{e_}")
                        hn = W13_SL[si_][1] - W13_SL[si_][0]
                        yW13[k % 3].inc(sy.dma_start(w13b[k % 3][:, 0:hn * 2048],
                                                     w13_ap(e_, si_)), 16)
                        ck3[("w13ld", k)] = yW13[k % 3].n

                w2q = [(f, mc, gi) for f in range(NQ) for mc in range(DC) for gi in range(3)]

                @blk.scalar
                def _(s):
                    yX1.wait(s)
                    yXn2.wait(s)
                    yL25[0].wait(s)
                    yL25[1].wait(s)
                    for gq in range(3):
                        f_, mc_, gi_ = w2q[gq]
                        cast_wait(s, f"w2_{mc_}")
                        yW2[gq % 3].inc(s.dma_start(w2b[gq % 3][:], w2_ap(mc_, gi_)), 16)
                        ck3[("w2ld", gq)] = yW2[gq % 3].n

                for f in range(NQ):
                    @blk.sync
                    def _(sy, f=f):
                        if f >= 1:
                            t3.wait(sy, ck3[("ymm", f - 1, DC - 1)])
                        yRwF.inc(sy.dma_start(
                            rwT_f[:], bass.AP(rws.tensor, f * 512, [[T, 8], [1, 512]])), 16)
                        if f >= 1:
                            t3.wait(sy, ck3[("sAdone", f - 1)])
                        yXn2f.inc(sy.dma_start(xn2f[:], xn2s[f]), 16)
                        if f >= 1:
                            v3.wait(sy, ck3[("ost", f - 1)])
                        yX1t.inc(sy.dma_start(
                            x1t[:],
                            bass.AP(x1s.tensor, f * 512 * DM,
                                    [[DM, 128], [128 * DM, 4], [1, DM]])), 16)

                    first_s = [True]
                    first_v = [True]
                    for e in range(NE):
                        if e < 8:
                            @blk.tensor
                            def _(tt, f=f, e=e):
                                if f == 0 and e == 0:
                                    yRwF.wait(tt, 16)
                                elif e == 0:
                                    yRwF.wait(tt, 16 * (f + 1))
                                if ("rwev", e - 1) in ck3:
                                    v3.wait(tt, ck3[("rwev", e - 1)])
                                elif f >= 1 and e == 0:
                                    v3.wait(tt, ck3[("rwev", 7)])
                                if f == 0 and e == 0:
                                    c0.wait(tt)
                                t3.inc(tt.matmul(psR[:], s_sel8[:, e, :], rwT_f[:],
                                                 start=True, stop=True))

                            @blk.vector
                            def _(v, f=f, e=e):
                                t3.wait(v)
                                v.drain()
                                v.tensor_copy(rwb[e % 2][:], psR[:])
                                v3.inc(v.drain())
                                ck3[("rwev", e)] = v3.n

                        for si in range(NSL):
                            k = sl_idx(f, e, si)
                            h0, h1 = W13_SL[si]
                            for hc in range(h0, h1):
                                hp = hc % 2
                                slot = hc % 4
                                grp = hc // 4
                                g0_, g1_ = grp * 4, min(grp * 4 + 4, HC)

                                @blk.tensor
                                def _(tt, f=f, e=e, si=si, hc=hc, hp=hp, k=k, h0=h0):
                                    if hc == h0:
                                        yW13[k % 3].wait(tt, ck3[("w13ld", k)])
                                        if e == 0 and si == 0:
                                            yXn2f.wait(tt, 16 * (f + 1))
                                    if ("ab_ev", hp) in ck3:
                                        v3.wait(tt, ck3[("ab_ev", hp)])
                                    base = (hc - h0) * 2048
                                    for dc in range(DC):
                                        tt.matmul(psA3[hp][:],
                                                  w13b[k % 3][:, base + dc * 128:base + dc * 128 + 128],
                                                  xn2f[:, dc, :],
                                                  start=(dc == 0), stop=(dc == DC - 1))
                                    for dc in range(DC):
                                        mm = tt.matmul(psB3[hp][:],
                                                       w13b[k % 3][:, base + 1024 + dc * 128:base + 1024 + dc * 128 + 128],
                                                       xn2f[:, dc, :],
                                                       start=(dc == 0), stop=(dc == DC - 1))
                                    t3.inc(mm)
                                    ck3[("ab", hp)] = t3.n
                                    if hc == h1 - 1:
                                        ck3[("w13use", k)] = t3.n

                                @blk.vector
                                def _(v, f=f, e=e, hc=hc, hp=hp, slot=slot, grp=grp):
                                    t3.wait(v, ck3[("ab", hp)])
                                    if first_v[0]:
                                        first_v[0] = False
                                        if f >= 1:
                                            t3.wait(v, ck3[("ydone", f - 1)])
                                    if ("silu_g", e, grp - 1) in ck3:
                                        s3.wait(v, ck3[("silu_g", e, grp - 1)])
                                    elif grp == 0 and ("silu_last",) in ck3:
                                        s3.wait(v, ck3[("silu_last",)])
                                    v.tensor_scalar(sA[hp][:], psA3[hp][:],
                                                    s_b13[:, e, hc, 0:1], None, AluOpType.add)
                                    v.drain()
                                    v3.inc(v.scalar_tensor_tensor(
                                        sProd[:, slot, :], psB3[hp][:], s_b13[:, e, hc, 1:2],
                                        sA[hp][:], AluOpType.add, AluOpType.mult))
                                    ck3[("ab_ev", hp)] = v3.n
                                    ck3[("pr", e, hc)] = v3.n

                                if hc == g1_ - 1:
                                    ng = g1_ - g0_

                                    @blk.scalar
                                    def _(s, f=f, e=e, grp=grp, g0_=g0_, g1_=g1_, ng=ng):
                                        v3.wait(s, ck3[("pr", e, g1_ - 1)])
                                        if first_s[0]:
                                            first_s[0] = False
                                            if f >= 1:
                                                t3.wait(s, ck3[("ydone", f - 1)])
                                        if ("hsc_last",) in ck3:
                                            v3.wait(s, ck3[("hsc_last",)])
                                        if e == 8:
                                            mm = s.activation(
                                                h_all[:, e, g0_:g1_, :].rearrange("p g d -> p (g d)"),
                                                sProd[:, 0:ng, :].rearrange("p g d -> p (g d)"),
                                                AF.Silu)
                                        else:
                                            mm = s.activation(
                                                htmp4[:, 0:ng * 512],
                                                sProd[:, 0:ng, :].rearrange("p g d -> p (g d)"),
                                                AF.Silu)
                                        s3.inc(mm)
                                        ck3[("silu_g", e, grp)] = s3.n
                                        ck3[("silu_last",)] = s3.n

                                    if e < 8:
                                        @blk.vector
                                        def _(v, f=f, e=e, grp=grp, g0_=g0_, g1_=g1_, ng=ng):
                                            s3.wait(v, ck3[("silu_g", e, grp)])
                                            v3.wait(v, ck3[("rwev", e)])
                                            rwb_b = bass.AP(
                                                rwb[e % 2].ap().tensor,
                                                rwb[e % 2].ap().offset,
                                                [rwb[e % 2].ap().ap[0], [0, ng], [1, 512]])
                                            v3.inc(v.tensor_tensor(
                                                h_all[:, e, g0_:g1_, :],
                                                htmp4[:, 0:ng * 512].rearrange(
                                                    "p (g d) -> p g d", g=ng),
                                                rwb_b, AluOpType.mult))
                                            ck3[("hsc_last",)] = v3.n

                            if k + 3 < n_sl:
                                f2, e2, si2 = sl_of(k + 3)

                                @blk.sync
                                def _(sy, k=k, f2=f2, e2=e2, si2=si2):
                                    t3.wait(sy, ck3[("w13use", k)])
                                    cast_wait(sy, f"w13_{e2}")
                                    hn = W13_SL[si2][1] - W13_SL[si2][0]
                                    yW13[k % 3].inc(sy.dma_start(
                                        w13b[(k + 3) % 3][:, 0:hn * 2048],
                                        w13_ap(e2, si2)), 16)
                                    ck3[("w13ld", k + 3)] = yW13[k % 3].n

                    ck3[("sAdone", f)] = t3.n
                    ck3[("hready", f)] = (s3.n, v3.n)

                    for mc in range(DC):
                        qpos = (f * DC + mc) * 3
                        for gi, (e0, e1) in enumerate(W2_GR):
                            gq = qpos + gi

                            @blk.tensor
                            def _(tt, f=f, mc=mc, gi=gi, e0=e0, e1=e1, gq=gq):
                                if gi == 0:
                                    sck, vck = ck3[("hready", f)]
                                    s3.wait(tt, sck)
                                    v3.wait(tt, vck)
                                    if f == 0 and mc == 0:
                                        cast_wait(tt, "b2")
                                    if ("yevac", f, mc - 2) in ck3:
                                        s3.wait(tt, ck3[("yevac", f, mc - 2)])
                                    elif f >= 1 and mc < 2:
                                        s3.wait(tt, ck3[("yevac", f - 1, DC - 2 + mc)])
                                yW2[gq % 3].wait(tt, ck3[("w2ld", gq)])
                                for e in range(e0, e1):
                                    for hc in range(HC):
                                        mm = tt.matmul(
                                            psY[mc % 2][:],
                                            w2b[gq % 3][:, ((e - e0) * HC + hc) * 128:
                                                        ((e - e0) * HC + hc) * 128 + 128],
                                            h_all[:, e, hc, :],
                                            start=(gi == 0 and e == e0 and hc == 0), stop=False)
                                if gi == 2:
                                    mm = tt.matmul(psY[mc % 2][:],
                                                   s_b2[:, mc * 128:(mc + 1) * 128],
                                                   rwT_f[:],
                                                   start=False, stop=True)
                                t3.inc(mm)
                                ck3[("w2use", gq)] = t3.n
                                if gi == 2:
                                    ck3[("ymm", f, mc)] = t3.n
                                    if mc == DC - 1:
                                        ck3[("ydone", f)] = t3.n

                            if gq + 3 < len(w2q):
                                f2, mc2, g2 = w2q[gq + 3]

                                @blk.scalar
                                def _(s, gq=gq, f2=f2, mc2=mc2, g2=g2):
                                    t3.wait(s, ck3[("w2use", gq)])
                                    cast_wait(s, f"w2_{mc2}")
                                    yW2[gq % 3].inc(s.dma_start(w2b[(gq + 3) % 3][:],
                                                                w2_ap(mc2, g2)), 16)
                                    ck3[("w2ld", gq + 3)] = yW2[gq % 3].n

                        @blk.scalar
                        def _(s, f=f, mc=mc):
                            t3.wait(s, ck3[("ymm", f, mc)])
                            s3.inc(s.activation(yf[mc % 2][:], psY[mc % 2][:], AF.Identity,
                                                bias=s_sb2[:, mc:mc + 1]))
                            ck3[("yevac", f, mc)] = s3.n

                        @blk.tensor
                        def _(tt, f=f, mc=mc):
                            s3.wait(tt, ck3[("yevac", f, mc)])
                            if ("ytev", f, mc - 1) in ck3:
                                v3.wait(tt, ck3[("ytev", f, mc - 1)])
                            elif f >= 1 and mc == 0:
                                v3.wait(tt, ck3[("ytev", f - 1, DC - 1)])
                            for sub in range(4):
                                mm = tt.transpose(psYT[:, sub * 128:(sub + 1) * 128],
                                                  yf[mc % 2][:, sub * 128:(sub + 1) * 128],
                                                  s_identf[:])
                            t3.inc(mm)
                            ck3[("yt", f, mc)] = t3.n

                        @blk.vector
                        def _(v, f=f, mc=mc):
                            t3.wait(v, ck3[("yt", f, mc)])
                            yX1t.wait(v, 16 * (f + 1))
                            k_out = f * DC + mc
                            if k_out >= 1:
                                yOut.wait(v, 64 * k_out)
                            v3.inc(v.tensor_tensor(
                                ost[0][:],
                                psYT[:].rearrange("p (s d) -> p s d", s=4),
                                bass.AP(x1t.ap().tensor, x1t.ap().offset + mc * 128,
                                        [x1t.ap().ap[0], [DM, 4], [1, 128]]),
                                AluOpType.add))
                            ck3[("ytev", f, mc)] = v3.n
                            if mc == DC - 1:
                                ck3[("ost", f)] = v3.n

                        @blk.sync
                        def _(sy, f=f, mc=mc):
                            v3.wait(sy, ck3[("ytev", f, mc)])
                            for sub in range(4):
                                yOut.inc(sy.dma_start(
                                    out_d[f * 512 + sub * 128:f * 512 + (sub + 1) * 128,
                                          mc * 128:(mc + 1) * 128],
                                    ost[0][:, sub, :]), 16)

                @blk.sync
                def _(sy):
                    yOut.wait(sy)

    return nc



def _prep_inputs(T, ins):
    NT = T // 128
    f32 = np.float32
    w1, w3, w2 = ins["L0_ffn_w1"], ins["L0_ffn_w3"], ins["L0_ffn_w2"]
    sw1, sw3, sw2 = ins["L0_ffn_sw1"], ins["L0_ffn_sw3"], ins["L0_ffn_sw2"]
    b1, b3, b2 = ins["L0_ffn_b1"], ins["L0_ffn_b3"], ins["L0_ffn_b2"]
    sb1, sb3, sb2 = ins["L0_ffn_sb1"], ins["L0_ffn_sb3"], ins["L0_ffn_sb2"]

    w1a = np.concatenate([w1, sw1[None]], 0)
    w3a = np.concatenate([w3, sw3[None]], 0)
    w13 = np.stack([w1a, w3a], 1)
    w13 = w13.reshape(NE, 2, DC, 128, HC, 128)
    w13 = np.ascontiguousarray(w13.transpose(0, 4, 3, 1, 2, 5))

    w2a = np.concatenate([w2, sw2[None]], 0)
    w2a = w2a.reshape(NE, HC, 128, DC, 128)
    w2a = np.ascontiguousarray(w2a.transpose(3, 2, 0, 1, 4))

    b1a = np.concatenate([b1, sb1[None]], 0)
    b3a = np.concatenate([b3, sb3[None]], 0)
    b13 = np.stack([b1a, b3a], -1).reshape(NE, HC, 128, 2)
    b13 = np.ascontiguousarray(b13.transpose(2, 0, 1, 3))

    gate_w = ins["L0_ffn_gate_w"]
    gwt = np.ascontiguousarray(gate_w.T.reshape(DC, 128, 8).transpose(1, 0, 2))
    gbb = np.ascontiguousarray(
        np.broadcast_to(np.asarray(ins["L0_ffn_gate_b"], f32)[None, :], (128, 8)))

    pos = np.arange(T, dtype=np.float64)
    freqs = np.power(10000.0, -np.arange(64, dtype=np.float64) / 64.0)
    rad = pos[:, None] * freqs[None, :]
    cosT = np.cos(rad).astype(ml_dtypes.bfloat16).reshape(NT, 128, 64)
    sinT = np.sin(rad).astype(ml_dtypes.bfloat16).reshape(NT, 128, 64)

    j = np.arange(128)[:, None]
    qq = np.arange(512)[None, :]
    masks = np.stack([(qq >= j + 128 * k) for k in range(4)], 0).astype(ml_dtypes.bfloat16)
    masks = np.ascontiguousarray(masks.transpose(1, 0, 2))

    ccols = np.zeros((128, 4), f32)
    ccols[:, 0] = EPS_LN
    ccols[:, 1] = EPS_RMS
    ccols[:, 2] = -1.0
    ccols[:, 3] = NEG_BIG

    return {
        "wq": np.ascontiguousarray(ins["L0_attn_wq"].reshape(DM, DM)),
        "wk": np.ascontiguousarray(ins["L0_attn_wk"].reshape(DM, DM)),
        "wv": np.ascontiguousarray(ins["L0_attn_wv"].reshape(DM, DM)),
        "wo": np.ascontiguousarray(ins["L0_attn_wo"]),
        "w13": w13, "w2": w2a, "b13": b13,
        "b2": np.ascontiguousarray(b2),
        "sb2c": np.ascontiguousarray(np.asarray(sb2, f32).reshape(DC, 128).T),
        "gwt": gwt, "gbb": gbb,
        "cosT": cosT, "sinT": sinT,
        "identb": np.eye(128, dtype=ml_dtypes.bfloat16),
        "identf": np.eye(128, dtype=f32),
        "ones1b": np.ones((1, 128), ml_dtypes.bfloat16),
        "onescb": np.ones((128, 1), ml_dtypes.bfloat16),
        "ones1f": np.ones((1, 128), f32),
        "masksb": masks,
        "ccols": ccols,
        "sel8": np.ascontiguousarray(
            np.eye(8, dtype=ml_dtypes.bfloat16)[:, :, None]
            * np.ones((1, 1, 128), ml_dtypes.bfloat16)),
    }


def kernel(**ins):
    x = np.asarray(ins["L0_attn_x"])
    B, T, _ = x.shape
    base = _prep_inputs(T, {k: np.asarray(v) for k, v in ins.items()})
    nc = build_nc(T)
    in_maps = []
    for b in range(B):
        m = dict(base)
        m["x"] = np.ascontiguousarray(x[b])
        in_maps.append(m)
    res = run_bass_kernel_spmd(nc, in_maps, core_ids=list(range(B)))
    return np.stack([np.asarray(r["out"]) for r in res.results], 0)
